# version 11
# baseline (speedup 1.0000x reference)
import contextlib
import numpy as np
import ml_dtypes
import concourse.bass as bass
import concourse.mybir as mybir
from concourse.bass_utils import run_bass_kernel_spmd

F32, BF16 = mybir.dt.float32, mybir.dt.bfloat16
AF = mybir.ActivationFunctionType
ALU = mybir.AluOpType
AX = mybir.AxisListType

D = 1024
H = 4
NG = 4
TPG = 512
NT = 4
E = 8
FQ = 1024
NQ = 4
PI2 = 1.5707963267948966


class Sched:
    ENG = ("pe", "act", "dve", "pool", "sp")

    def __init__(self, nc, ctx):
        self.nc = nc
        self.ctx = ctx
        self.ops = {e: [] for e in self.ENG}
        self.sem = {e: ctx.enter_context(nc.semaphore(name='sem_' + e)) for e in self.ENG}
        self.cnt = {e: 0 for e in self.ENG}
        self.maxwait = {e: {} for e in self.ENG}

    def _sem(self, name):
        if name not in self.sem:
            self.sem[name] = self.ctx.enter_context(
                self.nc.semaphore(name='sem_' + name))
            self.cnt[name] = 0
        return self.sem[name]

    def dma_event(self, stream):
        self._sem(stream)
        return (stream, self.cnt[stream])

    def emit(self, eng, fn, after=(), ms=False, dma=None):
        waits = []
        flat = []
        for ev in after:
            if ev is None:
                continue
            if isinstance(ev, list):
                flat += [e for e in ev if e is not None]
            else:
                flat.append(ev)
        for (pname, val) in flat:
            self._sem(pname)
            if val > self.maxwait[eng].get(pname, 0):
                waits.append((self.sem[pname], val))
                self.maxwait[eng][pname] = val
        inc = None
        if dma is not None:
            self._sem(dma)
            self.cnt[dma] += 16
            inc = (self.sem[dma], 16)
            self.ops[eng].append((fn, waits, inc))
            return (dma, self.cnt[dma])
        if ms:
            self.cnt[eng] += 1
            inc = (self.sem[eng], 1)
        self.ops[eng].append((fn, waits, inc))
        return (eng, self.cnt[eng])

    def run(self, eng, engine):
        for fn, waits, inc in self.ops[eng]:
            for (sem, val) in waits:
                engine.wait_ge(sem, val)
            inst = fn(engine)
            if inc is not None:
                inst.then_inc(inc[0], inc[1])


class Ring:
    def __init__(self, n):
        self.n = n
        self.last = [None] * n
        self.i = 0

    def acquire(self):
        s = self.i % self.n
        self.i += 1
        return s, self.last[s]

    def release(self, s, ev):
        self.last[s] = ev


class _TruncEx(Exception):
    pass


def build_program():
    def _stop(tag):
        if TRUNC is not None and TRUNC == tag:
            raise _TruncEx()
    nc = bass.Bass("TRN2", target_bir_lowering=False)

    xf = nc.dram_tensor("xf", [2048, D], F32, kind="ExternalInput").ap()
    xr = nc.dram_tensor("xr", [2048, D], F32, kind="ExternalInput").ap()
    wq_p = nc.dram_tensor("wq_p", [8, 128, 8, 512], F32, kind="ExternalInput").ap()
    wk_p = nc.dram_tensor("wk_p", [8, 128, 8, 512], F32, kind="ExternalInput").ap()
    wv_p = nc.dram_tensor("wv_p", [8, 128, 8, 512], F32, kind="ExternalInput").ap()
    wo_h = nc.dram_tensor("wo_h", [4096, D], F32, kind="ExternalInput").ap()
    w1t = nc.dram_tensor("w1t", [E, D, 4096], F32, kind="ExternalInput").ap()
    w2t = nc.dram_tensor("w2t", [E, 4096, D], F32, kind="ExternalInput").ap()
    wrt_h = nc.dram_tensor("wrt_h", [128, 64], F32, kind="ExternalInput").ap()
    phct_h = nc.dram_tensor("phct_h", [128, 32], F32, kind="ExternalInput").ap()
    bqc_h = nc.dram_tensor("bqc_h", [128, 32], F32, kind="ExternalInput").ap()
    bkc_h = nc.dram_tensor("bkc_h", [128, 32], F32, kind="ExternalInput").ap()
    bvc_h = nc.dram_tensor("bvc_h", [128, 32], F32, kind="ExternalInput").ap()
    bo_h = nc.dram_tensor("bo_h", [1, D], F32, kind="ExternalInput").ap()
    sel_h = nc.dram_tensor("sel_h", [128, 256], BF16, kind="ExternalInput").ap()
    eyeb_h = nc.dram_tensor("eyeb_h", [128, 128], BF16, kind="ExternalInput").ap()
    eyef_h = nc.dram_tensor("eyef_h", [128, 128], F32, kind="ExternalInput").ap()
    ones_h = nc.dram_tensor("ones_h", [1, 128], BF16, kind="ExternalInput").ap()
    s_scr = nc.dram_tensor("s_scr", [2048, D], BF16, kind="Internal").ap()
    out_d = nc.dram_tensor("out", [2048, D], F32, kind="ExternalOutput").ap()

    ctx = contextlib.ExitStack()
    with ctx:
        sb = lambda name, shape, dt: ctx.enter_context(nc.sbuf_tensor(name, shape, dt))
        psa = lambda name, shape, dt: ctx.enter_context(nc.psum_tensor(name, shape, dt))
        S = Sched(nc, ctx)

        AB = sb("AB", [128, 40960], BF16)
        AFr = sb("AFr", [128, 16384], F32)

        x_st = [sb(f"x_st{i}", [128, D], F32) for i in range(2)]
        xn_bf = []
        xnT_f = sb("xnT_f", [128, 4096], BF16)
        wpan = [sb(f"wpan{i}", [128, 4096], BF16) for i in range(2)]
        wor = [sb(f"wor{i}", [128, 2048], BF16) for i in range(2)]
        prod = [sb(f"prod{i}", [128, 1024], BF16) for i in range(2)]
        s_stg = [sb(f"s_stg{i}", [128, D], BF16) for i in range(2)]
        vt_st = [sb(f"vt_st{i}", [128, 512], BF16) for i in range(2)]
        stats = sb("stats", [128, 64], F32)
        consts = sb("consts", [128, 8], F32)
        phcc = sb("phcc", [128, 32], F32)
        psct = sb("psct", [128, 32], F32)
        bqc = sb("bqc", [128, 32], F32)
        bkc = sb("bkc", [128, 32], F32)
        bvc = sb("bvc", [128, 32], F32)
        wcl = sb("wcl", [128, 128], F32)
        wtmp = sb("wtmp", [128, 8], F32)
        sco = sb("sco", [16, 512], F32)
        lgt = sco[0:8, :]
        atn = sb("atn", [128, 64], F32)
        smx = sb("smx", [128, 16], F32)
        zs = sb("zs", [128, 16], F32)
        zr = sb("zr", [128, 16], F32)
        lgtm = sb("lgtm", [128, 32], F32)
        gate = sb("gate", [128, 128], F32)
        wrt_s = sb("wrt_s", [128, 64], BF16)
        bo_bf = sb("bo_bf", [1, D], BF16)
        sel_s = sb("sel_s", [128, 256], BF16)
        eyeb = sb("eyeb", [128, 128], BF16)
        eyef = sb("eyef", [128, 128], F32)
        ones_r = sb("ones_r", [1, 128], BF16)
        gct = sb("gct", [128, 16], F32)

        PD = [psa(f"PD{i}", [128, 1024], F32) for i in range(3)]
        PB3 = psa("PB3", [128, 2048], BF16)
        bank_ev = {}

        def bank_guard(key):
            return bank_ev.get(key, [])

        ld = []
        ld.append(S.emit("pool", lambda e: e.dma_start(out=wrt_s[:], in_=wrt_h[:, :]), dma="miscp"))
        ld.append(S.emit("sp", lambda e: e.dma_start(out=phcc[:], in_=phct_h[:, :]), dma="misc"))
        ld.append(S.emit("sp", lambda e: e.dma_start(out=bqc[:], in_=bqc_h[:, :]), dma="misc"))
        ld.append(S.emit("sp", lambda e: e.dma_start(out=bkc[:], in_=bkc_h[:, :]), dma="misc"))
        ld.append(S.emit("sp", lambda e: e.dma_start(out=bvc[:], in_=bvc_h[:, :]), dma="misc"))
        ld.append(S.emit("sp", lambda e: e.dma_start(out=sel_s[:], in_=sel_h[:, :]), dma="misc"))
        ld.append(S.emit("sp", lambda e: e.dma_start(out=eyeb[:], in_=eyeb_h[:, :]), dma="misc"))
        ld.append(S.emit("sp", lambda e: e.dma_start(out=eyef[:], in_=eyef_h[:, :]), dma="misc"))
        ld.append(S.emit("sp", lambda e: e.dma_start(out=ones_r[:], in_=ones_h[:, :]), dma="misc"))
        e_bo = S.emit("pool", lambda e: e.dma_start(out=bo_bf[:], in_=bo_h[:, :]), dma="miscp")
        e_misc = [S.dma_event("misc"), S.dma_event("miscp")]

        cvals = [PI2, -1.0 / D, 1.0 / D, 1e-5, -0.5, 1.0 / 32.0, 1.0, 0.0]
        e_c = None
        for i, v in enumerate(cvals):
            e_c = S.emit("dve", lambda e, i=i, v=v: e.memset(consts[:, i:i + 1], float(v)),
                         after=[e_c] if e_c else [], ms=True)

        e_sin = S.emit("act", lambda e: e.activation(psct[:], phcc[:], AF.Sin),
                       after=[e_misc, e_c], ms=True)
        e_cos = S.emit("act", lambda e: e.activation(phcc[:], phcc[:], AF.Sin, bias=consts[:, 0:1]),
                       after=[e_sin], ms=True)
        e_w = e_cos
        for pr in range(16):
            hh, gg2 = pr // 4, pr % 4
            e_w1_ = S.emit("dve", lambda e, hh=hh, gg2=gg2, pr=pr: e.tensor_mul(
                wcl[:, pr * 8:pr * 8 + 8], phcc[:, hh * 8:hh * 8 + 8],
                phcc[:, gg2 * 8:gg2 * 8 + 8]), after=[e_w], ms=True)
            e_w2_ = S.emit("dve", lambda e, hh=hh, gg2=gg2: e.tensor_mul(
                wtmp[:], psct[:, hh * 8:hh * 8 + 8], psct[:, gg2 * 8:gg2 * 8 + 8]),
                after=[e_w1_], ms=True)
            e_w = S.emit("dve", lambda e, pr=pr: e.tensor_add(
                wcl[:, pr * 8:pr * 8 + 8], wcl[:, pr * 8:pr * 8 + 8], wtmp[:]),
                after=[e_w2_], ms=True)
        e_trig = [e_w]

        ln_chain = [None]
        ln_slot_ev = {}
        ln_slot_ctr = [0]
        ln_sq_ev = [None]

        def emit_ln(x_ev, x_slot, xn_slot, extra_after=()):
            sl = ln_slot_ctr[0] % 8
            ln_slot_ctr[0] += 1
            c0 = sl * 8
            after0 = [x_ev, e_c, ln_slot_ev.get(sl)] + list(extra_after)
            e1 = S.emit("dve", lambda e: e.reduce_sum(stats[:, c0:c0 + 1], x_st[x_slot][:],
                                                      AX.X), after=after0, ms=True)
            e2 = S.emit("act", lambda e: e.activation(stats[:, c0 + 1:c0 + 2],
                                                      stats[:, c0:c0 + 1], AF.Identity,
                                                      scale=consts[:, 1:2]), after=[e1], ms=True)
            e3 = S.emit("dve", lambda e: e.tensor_scalar_add(x_st[x_slot][:], x_st[x_slot][:],
                                                             stats[:, c0 + 1:c0 + 2]),
                        after=[e2], ms=True)
            e4 = S.emit("act", lambda e: e.activation(sqd, x_st[x_slot][:], AF.Square,
                                                      accum_out=stats[:, c0 + 2:c0 + 3]),
                        after=[e3, ln_sq_ev[0]], ms=True)
            ln_sq_ev[0] = e4
            e5 = S.emit("act", lambda e: e.activation(stats[:, c0 + 3:c0 + 4],
                                                      stats[:, c0 + 2:c0 + 3], AF.Ln,
                                                      scale=consts[:, 2:3], bias=consts[:, 3:4]),
                        after=[e4], ms=True)
            e6 = S.emit("act", lambda e: e.activation(stats[:, c0 + 4:c0 + 5],
                                                      stats[:, c0 + 3:c0 + 4], AF.Exp,
                                                      scale=consts[:, 4:5]), after=[e5], ms=True)
            e7 = S.emit("dve", lambda e: e.tensor_scalar_mul(xn_bf[xn_slot][:],
                                                             x_st[x_slot][:],
                                                             stats[:, c0 + 4:c0 + 5]),
                        after=[e6], ms=True)
            ln_slot_ev[sl] = e7
            ln_chain[0] = e7
            return e7

        def emit_xn_transpose(xn_ev, xn_slot, dstT_ap, cstride, dst_col, extra_after=()):
            evs = []
            for cc in range(2):
                half = cc
                key = ("PD2", half)
                aft = [xn_ev] + list(extra_after) + bank_guard(key)
                last = None
                for c in range(4 * cc, 4 * cc + 4):
                    off = half * 512 + (c % 4) * 128
                    last = S.emit("pe", lambda e, c=c, off=off: e.transpose(
                        PD[2][:, off:off + 128], xn_bf[xn_slot][:, c * 128:(c + 1) * 128],
                        eyef[:]), after=aft, ms=(c == 4 * cc + 3))
                    aft = []
                ev = S.emit("act", lambda e, cc=cc, half=half: e.copy(
                    dstT_ap.rearrange("p (c t) -> p c t", c=8 * (4096 // cstride) // 8 if False
                                      else (4096 * 8 // (cstride * 8)) if False else 8)
                    [:, 4 * cc:4 * cc + 4, dst_col:dst_col + 128]
                    if cstride * 8 == dstT_ap.shape[-1] else None,
                    PD[2][:, half * 512:half * 512 + 512].rearrange("p (c t) -> p c t", c=4)),
                    after=[last], ms=True)
                bank_ev[key] = [ev]
                evs.append(ev)
            return evs[-1]

        def emit_xn_transpose2(xn_ev, xn_slot, dstT_ap, cstride, dst_col, extra_after=()):
            evs = []
            dst3 = dstT_ap.rearrange("p (c t) -> p c t", t=cstride)
            for cc in range(2):
                half = cc
                key = ("PD2", half)
                aft = [xn_ev] + list(extra_after) + bank_guard(key)
                last = None
                for c in range(4 * cc, 4 * cc + 4):
                    off = half * 512 + (c % 4) * 128
                    last = S.emit("pe", lambda e, c=c, off=off: e.transpose(
                        PD[2][:, off:off + 128], xn_bf[xn_slot][:, c * 128:(c + 1) * 128],
                        eyef[:]), after=aft, ms=(c == 4 * cc + 3))
                    aft = []
                ev = S.emit("act", lambda e, cc=cc, half=half: e.copy(
                    dst3[:, 4 * cc:4 * cc + 4, dst_col:dst_col + 128],
                    PD[2][:, half * 512:half * 512 + 512].rearrange("p (c t) -> p c t", c=4)),
                    after=[last], ms=True)
                bank_ev[key] = [ev]
                evs.append(ev)
            return evs[-1]

        def proj_chunk(nc_idx, pan_slot, pan_ev, xnT_ap, xnT_ev, pp):
            key = ("PD0", pp)
            aft = [pan_ev, xnT_ev] + bank_guard(key)
            last = None
            for dc in range(8):
                lhs = wpan[pan_slot][:, dc * 512 + (nc_idx % 4) * 128:
                                     dc * 512 + (nc_idx % 4) * 128 + 128]
                last = S.emit("pe", lambda e, lhs=lhs, dc=dc, pp=pp: e.matmul(
                    PD[0][:, pp * 512:pp * 512 + 512], lhs,
                    xnT_ap[:, dc * 512:dc * 512 + 512],
                    start=(dc == 0), stop=(dc == 7)),
                    after=aft if dc == 0 else [], ms=(dc == 7))
                aft = []
            return last

        try:
            kbT = AB[:, 0:16384]
            qbT = AB[:, 32768:36864]
            qw_sl = [AB[:, 36864:37888], AB[:, 37888:38912]]
            v_tm = AB[:, 0:16384]
            outT = AB[:, 16384:32768]
            out_tm = [AFr[:, 0:4096], AFr[:, 4096:8192]]
            out_bf = [AB[:, 32768:36864], AB[:, 36864:40960]]
            sqd = AFr[:, 8192:9216]
            xn_bf.append(AFr[:, 9216:10240])
            xn_bf.append(AFr[:, 10240:11264])

            pan_ring = Ring(2)
            prod_sl = [prod[0][:], prod[1][:]]
            prod_ev = {i: None for i in range(2)}
            wo_ring = Ring(2)
            prod_ring = Ring(2)
            x_ring = Ring(2)
            xn_ring = Ring(2)
            vt_ring = Ring(2)
            s_ring = Ring(2)

            e_spill_all = []
            group_wo_done = None
            e_xnT_ffn_evs = []

            sc_state = {"first": True, "aft0": None, "last": None}

            def emit_front_ln(g):
                last = None
                for tt in range(NT):
                    slot, rel = x_ring.acquire()
                    e_x = S.emit("sp", lambda e, slot=slot, g=g, tt=tt: e.dma_start(
                        out=x_st[slot][:], in_=xf[g * TPG + tt * 128: g * TPG + (tt + 1) * 128, :]),
                        after=[rel], dma=f"x{slot}")
                    xnslot, xnrel = xn_ring.acquire()
                    e_ln = emit_ln(e_x, slot, xnslot, extra_after=[xnrel] + e_trig)
                    x_ring.release(slot, e_ln)
                    e_tr = emit_xn_transpose2(e_ln, xnslot, xnT_f[:], 512, tt * 128)
                    xn_ring.release(xnslot, e_tr)
                    last = e_tr
                return last

            for g in range(NG):
                e_xnT = emit_front_ln(g)
                _stop(f"g{g}_ln")

                k_ev_last = []
                for pan in range(8):
                    pslot, prel = pan_ring.acquire()
                    e_pan = S.emit("pool", lambda e, pslot=pslot, pan=pan: e.dma_start(
                        out=wpan[pslot][:].rearrange("p (c n) -> p c n", c=8), in_=wk_p[pan]),
                        after=[prel], dma=f"pan{pslot}")
                    lastmm = None
                    for sub in range(4):
                        nc_idx = pan * 4 + sub
                        pp = nc_idx % 2
                        e_mm = proj_chunk(nc_idx, pslot, e_pan, xnT_f[:], e_xnT, pp)
                        if nc_idx % 2 == 0:
                            e_v1 = S.emit("dve", lambda e, nc_idx=nc_idx, pp=pp:
                                          e.tensor_scalar_add(
                                kbT[:, nc_idx * 512:nc_idx * 512 + 512],
                                PD[0][:, pp * 512:pp * 512 + 512],
                                bkc[:, nc_idx:nc_idx + 1]), after=[e_mm], ms=True)
                        else:
                            e_v1 = S.emit("act", lambda e, nc_idx=nc_idx, pp=pp: e.activation(
                                kbT[:, nc_idx * 512:nc_idx * 512 + 512],
                                PD[0][:, pp * 512:pp * 512 + 512], AF.Identity,
                                bias=bkc[:, nc_idx:nc_idx + 1]), after=[e_mm], ms=True)
                        bank_ev[("PD0", pp)] = [e_v1]
                        lastmm = e_mm
                        k_ev_last = [e_v1]
                    pan_ring.release(pslot, lastmm)
                _stop(f"g{g}_k")

                for h in range(H):
                    q_ev_last = []
                    lastmm = {}
                    pans = {}
                    for hp in range(2):
                        pslot, prel = pan_ring.acquire()
                        e_pan = S.emit("pool", lambda e, pslot=pslot, h=h, hp=hp: e.dma_start(
                            out=wpan[pslot][:].rearrange("p (c n) -> p c n", c=8),
                            in_=wq_p[2 * h + hp]), after=[prel], dma=f"pan{pslot}")
                        pans[hp] = (pslot, e_pan)
                    e_pan_all = [S.dma_event("pan0"), S.dma_event("pan1")]
                    q_ev_byc = {}
                    for sub in range(8):
                        nc_idx = h * 8 + sub
                        pslot, e_pan = pans[sub // 4]
                        pp = nc_idx % 2
                        e_mm = proj_chunk(nc_idx, pslot, e_pan_all, xnT_f[:], e_xnT, pp)
                        lastmm[sub // 4] = e_mm
                        e_q2 = S.emit("act", lambda e, nc_idx=nc_idx, sub=sub, pp=pp: e.activation(
                            qbT[:, sub * 512:sub * 512 + 512],
                            PD[0][:, pp * 512:pp * 512 + 512], AF.Identity,
                            bias=bqc[:, nc_idx:nc_idx + 1]), after=[e_mm], ms=True)
                        bank_ev[("PD0", pp)] = [e_q2]
                        q_ev_byc[sub] = e_q2
                        q_ev_last = [e_q2]
                    pan_ring.release(pans[0][0], lastmm[0])
                    pan_ring.release(pans[1][0], lastmm[1])
                    qw_ev = {0: None, 1: None}
                    for gg in range(H):
                        pair = h * 4 + gg
                        for cp in range(4):
                            qws = cp % 2
                            e_qw = S.emit("dve", lambda e, qws=qws, cp=cp, pair=pair:
                                          e.tensor_scalar_mul(
                                qw_sl[qws][:, 0:512],
                                qbT[:, cp * 1024:cp * 1024 + 512],
                                wcl[:, pair * 8 + 2 * cp:pair * 8 + 2 * cp + 1]),
                                after=[q_ev_byc[2 * cp], qw_ev[qws]], ms=True)
                            e_qw = S.emit("dve", lambda e, qws=qws, cp=cp, pair=pair:
                                          e.tensor_scalar_mul(
                                qw_sl[qws][:, 512:1024],
                                qbT[:, cp * 1024 + 512:cp * 1024 + 1024],
                                wcl[:, pair * 8 + 2 * cp + 1:pair * 8 + 2 * cp + 2]),
                                after=[q_ev_byc[2 * cp + 1], e_qw], ms=True)
                            prslot = cp % 2
                            prrel = prod_ev[prslot]
                            e_pr = S.emit("dve", lambda e, prslot=prslot, qws=qws, cp=cp,
                                          gg=gg: e.tensor_mul(
                                prod_sl[prslot],
                                qw_sl[qws][:],
                                kbT[:, (gg * 8 + 2 * cp) * 512:(gg * 8 + 2 * cp) * 512 + 1024]),
                                after=[e_qw, k_ev_last, prrel], ms=True)
                            qw_ev[qws] = e_pr
                            e_smm = None
                            for half in range(2):
                                aftm = [e_pr, e_misc]
                                if sc_state["first"] and sc_state["aft0"] is not None:
                                    aftm.append(sc_state["aft0"])
                                is_last = (pair == 15 and cp == 3 and half == 1)
                                e_smm = S.emit("pe", lambda e, prslot=prslot, half=half,
                                               pair=pair, st=sc_state["first"],
                                               sp_=is_last: e.matmul(
                                    PD[1][0:16, 0:512],
                                    sel_s[:, pair * 16:pair * 16 + 16],
                                    prod_sl[prslot][:, half * 512:half * 512 + 512],
                                    start=st, stop=sp_),
                                    after=aftm, ms=(half == 1))
                                sc_state["first"] = False
                            prod_ev[prslot] = e_smm
                            sc_state["last"] = e_smm

                _stop(f"g{g}_q")
                e_sc = S.emit("dve", lambda e: e.tensor_scalar_mul(sco[:], PD[1][0:16, 0:512],
                                                                   consts[0:16, 5:6]),
                              after=[sc_state["last"], e_c], ms=True)
                sc_state["first"] = True
                sc_state["aft0"] = e_sc
                atn_evs = []
                for tt in range(NT):
                    key = ("PD1T",)
                    aft = [e_sc] + bank_guard(key)
                    e_t = S.emit("pe", lambda e, tt=tt: e.transpose(
                        PD[1][:, 512 + tt * 16:512 + tt * 16 + 16],
                        sco[:, tt * 128:tt * 128 + 128], eyef[0:16, 0:16]), after=aft, ms=True)
                    e_a = S.emit("dve", lambda e, tt=tt: e.tensor_copy(
                        atn[:, tt * 16:tt * 16 + 16], PD[1][:, 512 + tt * 16:512 + tt * 16 + 16]),
                        after=[e_t], ms=True)
                    bank_ev[key] = [e_a]
                    atn_evs.append(e_a)
                smax_evs = []
                for tt in range(NT):
                    for h in range(H):
                        c0 = tt * 16 + h * 4
                        e_m = S.emit("dve", lambda e, c0=c0, tt=tt, h=h: e.reduce_max(
                            smx[:, tt * 4 + h:tt * 4 + h + 1], atn[:, c0:c0 + 4], AX.X, negate=True),
                            after=[atn_evs[tt]], ms=True)
                        e_e = S.emit("act", lambda e, c0=c0, tt=tt, h=h: e.activation(
                            atn[:, c0:c0 + 4], atn[:, c0:c0 + 4], AF.Exp,
                            bias=smx[:, tt * 4 + h:tt * 4 + h + 1],
                            accum_out=zs[:, tt * 4 + h:tt * 4 + h + 1]), after=[e_m], ms=True)
                        smax_evs.append(e_e)
                e_zr = S.emit("dve", lambda e: e.reciprocal(zr[:], zs[:]), after=smax_evs, ms=True)
                e_attn = None
                for tt in range(NT):
                    for h in range(H):
                        c0 = tt * 16 + h * 4
                        e_attn = S.emit("dve", lambda e, c0=c0, tt=tt, h=h: e.tensor_scalar_mul(
                            atn[:, c0:c0 + 4], atn[:, c0:c0 + 4],
                            zr[:, tt * 4 + h:tt * 4 + h + 1]), after=[e_zr], ms=True)

                _stop(f"g{g}_smax")
                v_last = None
                for pan in range(8):
                    pslot, prel = pan_ring.acquire()
                    e_pan = S.emit("pool", lambda e, pslot=pslot, pan=pan: e.dma_start(
                        out=wpan[pslot][:].rearrange("p (c n) -> p c n", c=8), in_=wv_p[pan]),
                        after=[prel], dma=f"pan{pslot}")
                    lastmm = None
                    for sub in range(4):
                        nc_idx = pan * 4 + sub
                        pp = nc_idx % 2
                        e_mm = proj_chunk(nc_idx, pslot, e_pan, xnT_f[:], e_xnT, pp)
                        lastmm = e_mm
                        vslot, vrel = vt_ring.acquire()
                        e_ve = S.emit("act", lambda e, nc_idx=nc_idx, pp=pp, vslot=vslot: e.activation(
                            vt_st[vslot][:], PD[0][:, pp * 512:pp * 512 + 512], AF.Identity,
                            bias=bvc[:, nc_idx:nc_idx + 1]), after=[e_mm, vrel], ms=True)
                        bank_ev[("PD0", pp)] = [e_ve]
                        half = nc_idx % 2
                        keyt = ("PB3", half)
                        aftt = [e_ve] + bank_guard(keyt)
                        last = None
                        for b4 in range(4):
                            last = S.emit("pe", lambda e, b4=b4, half=half, vslot=vslot: e.transpose(
                                PB3[:, half * 1024 + b4 * 128:half * 1024 + b4 * 128 + 128],
                                vt_st[vslot][:, b4 * 128:b4 * 128 + 128], eyeb[:]),
                                after=aftt, ms=(b4 == 3))
                            aftt = []
                        vt_ring.release(vslot, last)
                        e_vt = S.emit("dve", lambda e, nc_idx=nc_idx, half=half: e.tensor_copy(
                            v_tm.rearrange("p (t n) -> p t n", t=4)[:, :,
                                                                    nc_idx * 128:nc_idx * 128 + 128],
                            PB3[:, half * 1024:half * 1024 + 512].rearrange("p (t n) -> p t n", t=4)),
                            after=[last], ms=True)
                        bank_ev[keyt] = [e_vt]
                        v_last = e_vt
                    pan_ring.release(pslot, lastmm)

                _stop(f"g{g}_v")
                out_ev_tiles = []
                for tt in range(NT):
                    obf = out_bf[tt % 2]
                    prev_ot = out_ev_tiles[tt - 2] if tt >= 2 else None
                    ao_evs = []
                    for h in range(H):
                        e_ao = S.emit("dve", lambda e, tt=tt, h=h, obf=obf: e.tensor_scalar_mul(
                            obf[:, h * 1024:h * 1024 + 1024],
                            v_tm[:, tt * 4096:tt * 4096 + 1024],
                            atn[:, tt * 16 + h * 4:tt * 16 + h * 4 + 1]),
                            after=[v_last, e_attn, prev_ot], ms=True)
                        for gg in range(1, 4):
                            e_ao = S.emit("dve", lambda e, tt=tt, h=h, gg=gg, obf=obf:
                                          e.scalar_tensor_tensor(
                                obf[:, h * 1024:h * 1024 + 1024],
                                v_tm[:, tt * 4096 + gg * 1024:tt * 4096 + gg * 1024 + 1024],
                                atn[:, tt * 16 + h * 4 + gg:tt * 16 + h * 4 + gg + 1],
                                obf[:, h * 1024:h * 1024 + 1024],
                                ALU.mult, ALU.add), after=[e_ao], ms=True)
                        ao_evs.append(e_ao)
                    last_oe = None
                    for oc in range(8):
                        c0 = oc * 4
                        half = oc % 2
                        keyt = ("PB3", half)
                        aftt = [ao_evs[c0 // 8]] + bank_guard(keyt)
                        last = None
                        for b4 in range(4):
                            c = c0 + b4
                            last = S.emit("pe", lambda e, c=c, b4=b4, half=half, obf=obf:
                                          e.transpose(
                                PB3[:, half * 1024 + b4 * 128:half * 1024 + b4 * 128 + 128],
                                obf[:, c * 128:c * 128 + 128], eyeb[:]),
                                after=aftt, ms=(b4 == 3))
                            aftt = []
                        last_oe = S.emit("act", lambda e, c0=c0, half=half, tt=tt: e.copy(
                            outT.rearrange("p (c t) -> p c t", c=32)[:, c0:c0 + 4,
                                                                     tt * 128:tt * 128 + 128],
                            PB3[:, half * 1024:half * 1024 + 512]
                            .rearrange("p (c t) -> p c t", c=4)), after=[last], ms=True)
                        bank_ev[keyt] = [last_oe]
                    out_ev_tiles.append(last_oe)
                e_outT = out_ev_tiles[-1]

                _stop(f"g{g}_ao")
                aft_wo0 = [e_outT]
                for key in [("PD0", 0), ("PD0", 1), ("PD1T",), ("PD1S",), ("PD2", 0),
                            ("PD2", 1), ("PB3", 0), ("PB3", 1)]:
                    aft_wo0 += bank_guard(key)
                aft_wo0.append(sc_state["aft0"])
                wo_last = {}
                first_wo = True
                for nhf in range(2):
                    for kc4 in range(8):
                        wslot, wrel = wo_ring.acquire()
                        e_wo = S.emit("pool", lambda e, wslot=wslot, kc4=kc4, nhf=nhf:
                                      e.dma_start(
                            out=wor[wslot][:].rearrange("p (c n) -> p c n", c=4),
                            in_=wo_h[kc4 * 512:(kc4 + 1) * 512,
                                     nhf * 512:(nhf + 1) * 512]
                            .rearrange("(c p) n -> p c n", p=128)), after=[wrel],
                            dma=f"wo{wslot}")
                        lastuse = None
                        for sub in range(4):
                            kc = kc4 * 4 + sub
                            j, dr = kc // 8, kc % 8
                            for h in range(H):
                                lhs = outT[:, (h * 8 + dr) * 512 + j:
                                           (h * 8 + dr) * 512 + 512: 4]
                                aftm = [e_wo]
                                if first_wo:
                                    aftm += aft_wo0
                                    first_wo = False
                                if kc == 0 and nhf == 1:
                                    aftm.append(wo_last.get(h))
                                e_wm = S.emit("pe", lambda e, h=h, wslot=wslot, kc=kc,
                                              sub=sub, lhs=lhs: e.matmul(
                                    PD[h // 2][:, (h % 2) * 512:
                                               (h % 2) * 512 + 512], lhs,
                                    wor[wslot][:, sub * 512:sub * 512 + 512],
                                    start=(kc == 0), stop=False), after=aftm,
                                    ms=(h == 3))
                                lastuse = e_wm
                                wo_last[h] = e_wm
                        wo_ring.release(wslot, lastuse)
                    for h in range(H):
                        e_bo_mm = S.emit("pe", lambda e, h=h, nhf=nhf: e.matmul(
                            PD[h // 2][:, (h % 2) * 512:(h % 2) * 512 + 512], ones_r[:],
                            bo_bf[:, nhf * 512:nhf * 512 + 512], start=False, stop=True),
                            after=[e_misc, wo_last[h]], ms=True)
                        wo_last[h] = e_bo_mm
                        sslot, srel = s_ring.acquire()
                        e_se = S.emit("dve", lambda e, h=h, sslot=sslot: e.tensor_copy(
                            s_stg[sslot][:, 0:512],
                            PD[h // 2][:, (h % 2) * 512:(h % 2) * 512 + 512]),
                            after=[e_bo_mm, srel], ms=True)
                        e_sp = S.emit("sp", lambda e, h=h, g=g, sslot=sslot, nhf=nhf:
                                      e.dma_start(
                            out=s_scr[h * 512 + g * 128: h * 512 + (g + 1) * 128,
                                      nhf * 512:(nhf + 1) * 512],
                            in_=s_stg[sslot][:, 0:512]), after=[e_se], dma=f"s{sslot}")
                        s_ring.release(sslot, e_sp)
                        e_spill_all.append(e_sp)
                        if nhf == 1:
                            if h == 0:
                                bank_ev[("PD0", 0)] = [e_se]
                            elif h == 1:
                                bank_ev[("PD0", 1)] = [e_se]
                            elif h == 2:
                                bank_ev[("PD1S",)] = [e_se]
                            else:
                                bank_ev[("PD1T",)] = [e_se]
                        wo_last[h] = e_se
                group_wo_done = wo_last[3]
                sc_state["aft0"] = bank_ev[("PD1T",)][0]
                bank_ev[("PD2", 0)] = bank_ev.get(("PD2", 0), [])
                bank_ev[("PD2", 1)] = bank_ev.get(("PD2", 1), [])
                _stop(f"g{g}_wo")

            xnT_ffn = AB[:, 0:16384]
            hT = AB[:, 16384:20480]
            w1s = [AB[:, 20480 + i * 8192: 20480 + (i + 1) * 8192] for i in range(2)]
            w2slots = [wpan[0][:, i * 1024:(i + 1) * 1024] for i in range(4)] + \
                      [wpan[1][:, i * 1024:(i + 1) * 1024] for i in range(4)] + \
                      [xnT_f[:, i * 1024:(i + 1) * 1024] for i in range(4)] + \
                      [AB[:, 36864 + i * 2048: 36864 + (i + 1) * 2048][:, 0:1024]
                       for i in range(2)] + [prod[0][:], prod[1][:]]
            acc = AFr[:, 0:16384]

            ffn_guard = [group_wo_done, e_spill_all[-1], e_attn]

            lgt_evs = []
            e_xnT_strip = {}
            e_gate_strip = {}
            for g in range(NG):
                xnT_evs = []
                for tt in range(NT):
                    slot, rel = x_ring.acquire()
                    e_x = S.emit("sp", lambda e, slot=slot, g=g, tt=tt: e.dma_start(
                        out=x_st[slot][:], in_=xr[g * TPG + tt * 128: g * TPG + (tt + 1) * 128, :]),
                        after=[rel], dma=f"x{slot}")
                    xnslot, xnrel = xn_ring.acquire()
                    e_ln = emit_ln(e_x, slot, xnslot, extra_after=[xnrel])
                    x_ring.release(slot, e_ln)
                    e_tr = emit_xn_transpose2(e_ln, xnslot, xnT_ffn, 2048, g * 512 + tt * 128,
                                              extra_after=ffn_guard)
                    xn_ring.release(xnslot, e_tr)
                    xnT_evs.append(e_tr)
                e_xnT_ffn_evs.append(xnT_evs[-1])
                e_xnT_strip[g] = xnT_evs[-1]
                key = ("PD1S",)
                aft = [xnT_evs[-1], e_misc] + bank_guard(key)
                last = None
                for dc in range(8):
                    last = S.emit("pe", lambda e, dc=dc, g=g: e.matmul(
                        PD[1][0:8, 0:512], wrt_s[:, dc * 8:dc * 8 + 8],
                        xnT_ffn[:, dc * 2048 + g * 512:dc * 2048 + g * 512 + 512],
                        start=(dc == 0), stop=(dc == 7)),
                        after=aft if dc == 0 else [], ms=(dc == 7))
                    aft = []
                e_le = S.emit("dve", lambda e: e.tensor_copy(lgt, PD[1][0:8, 0:512]),
                              after=[last], ms=True)
                bank_ev[key] = [e_le]
                for tt in range(NT):
                    keyt = ("PD1T",)
                    aftt = [e_le] + bank_guard(keyt)
                    e_t = S.emit("pe", lambda e, tt=tt: e.transpose(
                        PD[1][:, 512 + tt * 8:512 + tt * 8 + 8],
                        lgt[:, tt * 128:tt * 128 + 128], eyef[0:8, 0:8]), after=aftt, ms=True)
                    e_lt = S.emit("dve", lambda e, tt=tt: e.tensor_copy(
                        lgtm[:, tt * 8:tt * 8 + 8], PD[1][:, 512 + tt * 8:512 + tt * 8 + 8]),
                        after=[e_t], ms=True)
                    bank_ev[keyt] = [e_lt]
                    t16 = g * 4 + tt
                    e_gm = S.emit("dve", lambda e, tt=tt: e.reduce_max(
                        gct[:, 0:1], lgtm[:, tt * 8:tt * 8 + 8], AX.X, negate=True),
                        after=[e_lt], ms=True)
                    e_ge = S.emit("act", lambda e, tt=tt, t16=t16: e.activation(
                        gate[:, t16 * 8:t16 * 8 + 8], lgtm[:, tt * 8:tt * 8 + 8], AF.Exp,
                        bias=gct[:, 0:1]), after=[e_gm], ms=True)
                    e_v1 = S.emit("dve", lambda e, t16=t16: e.reduce_max(
                        gct[:, 1:2], gate[:, t16 * 8:t16 * 8 + 8], AX.X), after=[e_ge], ms=True)
                    e_msk = S.emit("dve", lambda e, t16=t16: e.tensor_single_scalar(
                        gct[:, 4:12], gate[:, t16 * 8:t16 * 8 + 8], gct[:, 1:2], ALU.is_lt),
                        after=[e_v1], ms=True)
                    e_em = S.emit("dve", lambda e, t16=t16: e.tensor_mul(
                        gct[:, 4:12], gct[:, 4:12], gate[:, t16 * 8:t16 * 8 + 8]),
                        after=[e_msk], ms=True)
                    e_v2 = S.emit("dve", lambda e: e.reduce_max(gct[:, 2:3], gct[:, 4:12], AX.X),
                                  after=[e_em], ms=True)
                    e_dn = S.emit("dve", lambda e: e.tensor_add(gct[:, 3:4], gct[:, 1:2],
                                                                gct[:, 2:3]), after=[e_v2], ms=True)
                    e_rd = S.emit("dve", lambda e: e.reciprocal(gct[:, 3:4], gct[:, 3:4]),
                                  after=[e_dn], ms=True)
                    e_gmask = S.emit("dve", lambda e, t16=t16: e.tensor_single_scalar(
                        gct[:, 4:12], gate[:, t16 * 8:t16 * 8 + 8], gct[:, 2:3], ALU.is_ge),
                        after=[e_rd], ms=True)
                    e_gsc = S.emit("dve", lambda e, t16=t16: e.tensor_scalar_mul(
                        gate[:, t16 * 8:t16 * 8 + 8], gate[:, t16 * 8:t16 * 8 + 8], gct[:, 3:4]),
                        after=[e_gmask], ms=True)
                    e_gf = S.emit("dve", lambda e, t16=t16: e.tensor_mul(
                        gate[:, t16 * 8:t16 * 8 + 8], gate[:, t16 * 8:t16 * 8 + 8], gct[:, 4:12]),
                        after=[e_gsc], ms=True)
                    lgt_evs.append(e_gf)
                    e_gate_strip[g] = e_gf
            e_gate = lgt_evs[-1]
            _stop("ffnln")
            e_xnT_all = e_xnT_ffn_evs[-1]

            acc_ev = {}
            for g in range(NG):
                aft_m = ffn_guard + [e_xnT_strip[g]]
                if g == 2:
                    aft_m += [ln_chain[0], e_xnT_all]
                e_acc0 = S.emit("dve", lambda e, g=g: e.memset(
                    acc[:, g * 4096:(g + 1) * 4096], 0.0), after=aft_m, ms=True)
                for ts in range(4):
                    acc_ev[g * 4 + ts] = e_acc0

            w1_ring = Ring(2)
            w2_ring = Ring(2)
            ybank = {0: bank_guard(("PD1S",)) + bank_guard(("PD1T",)),
                     1: bank_guard(("PD2", 0)) + bank_guard(("PD2", 1))}
            yrot = [0]
            for ex in range(E):
                for qq in range(NQ):
                    piece = ex * NQ + qq
                    pslot, prel = w1_ring.acquire()
                    w2par, w2rel = w2_ring.acquire()
                    w2evs = []
                    for fc in range(8):
                        aft2 = []
                        if fc == 0:
                            aft2 = [w2rel]
                            if piece == 0:
                                aft2 += ffn_guard
                        e_w2c = S.emit("pool", lambda e, w2par=w2par, ex=ex, qq=qq, fc=fc:
                                       e.dma_start(
                            out=w2slots[w2par * 8 + fc],
                            in_=w2t[ex, qq * FQ + fc * 128:qq * FQ + (fc + 1) * 128, :]),
                            after=aft2, dma=f"w2{w2par}")
                        w2evs.append(e_w2c)
                    e_w2 = w2evs[-1]
                    w1evs = []
                    for dc in range(8):
                        aft = []
                        if dc == 0:
                            aft = [prel]
                            if piece == 0:
                                aft += ffn_guard
                        e_w1 = S.emit("pool", lambda e, pslot=pslot, ex=ex, qq=qq, dc=dc: e.dma_start(
                            out=w1s[pslot][:, dc * 1024:(dc + 1) * 1024],
                            in_=w1t[ex, dc * 128:(dc + 1) * 128, qq * FQ:(qq + 1) * FQ]),
                            after=aft, dma=f"w1{pslot}")
                        w1evs.append(e_w1)
                    lastmm_piece = None
                    last_y_mm = [None]
                    for g in range(NG):
                        sil_evs = []
                        for fc in range(8):
                            pp = fc % 2
                            key = ("PD0", pp)
                            aft = [w1evs[-1], e_xnT_strip[g]] + bank_guard(key)
                            last = None
                            for dc in range(8):
                                lhs = w1s[pslot][:, dc * 1024 + fc * 128:dc * 1024 + fc * 128 + 128]
                                last = S.emit("pe", lambda e, lhs=lhs, dc=dc, pp=pp, g=g: e.matmul(
                                    PD[0][:, pp * 512:pp * 512 + 512], lhs,
                                    xnT_ffn[:, dc * 2048 + g * 512:dc * 2048 + g * 512 + 512],
                                    start=(dc == 0), stop=(dc == 7)),
                                    after=aft if dc == 0 else [], ms=(dc == 7))
                                aft = []
                            lastmm_piece = last
                            e_sil = S.emit("act", lambda e, fc=fc, pp=pp: e.activation(
                                hT[:, fc * 512:fc * 512 + 512], PD[0][:, pp * 512:pp * 512 + 512],
                                AF.Silu), after=[last], ms=True)
                            bank_ev[key] = [e_sil]
                            sil_evs.append(e_sil)
                        for pas in range(2):
                            ts_pair = (0, 1) if pas == 0 else (2, 3)
                            tsbank = {}
                            for ts in ts_pair:
                                tsbank[ts] = 1 + (yrot[0] % 2)
                                yrot[0] += 1
                            y_last = {}
                            for ts in ts_pair:
                                bi = tsbank[ts]
                                first = True
                                for fc in range(8):
                                    lhs = hT[:, fc * 512 + ts * 128:fc * 512 + ts * 128 + 128]
                                    for nh in range(2):
                                        aftm = [sil_evs[fc], e_w2]
                                        if first:
                                            aftm += ybank[bi - 1]
                                        e_ym = S.emit("pe", lambda e, bi=bi, nh=nh,
                                                      w2par=w2par, fc=fc, lhs=lhs,
                                                      st=(fc == 0),
                                                      sp_=(fc == 7): e.matmul(
                                            PD[bi][:, nh * 512:nh * 512 + 512], lhs,
                                            w2slots[w2par * 8 + fc][:, nh * 512:nh * 512 + 512],
                                            start=st, stop=sp_), after=aftm, ms=(nh == 1))
                                        first = False
                                    y_last[ts] = e_ym
                                    last_y_mm[0] = e_ym
                            for ts in ts_pair:
                                bi = tsbank[ts]
                                t16 = g * 4 + ts
                                e_ya = S.emit("dve", lambda e, bi=bi, t16=t16, ex=ex:
                                              e.scalar_tensor_tensor(
                                    acc[:, t16 * 1024:t16 * 1024 + 1024],
                                    PD[bi][:],
                                    gate[:, t16 * 8 + ex:t16 * 8 + ex + 1],
                                    acc[:, t16 * 1024:t16 * 1024 + 1024],
                                    ALU.mult, ALU.add),
                                    after=[y_last[ts], acc_ev[t16],
                                           e_gate_strip[g]], ms=True)
                                ybank[bi - 1] = [e_ya]
                                acc_ev[t16] = e_ya
                    w1_ring.release(pslot, lastmm_piece)
                    w2_ring.release(w2par, last_y_mm[0])
                _stop(f"p{piece}")

            e_expert_done = [last_y_mm[0]] + [acc_ev[t] for t in range(16)]
            e_spill_tot = [S.dma_event("s0"), S.dma_event("s1")]
            out_evs = []
            for t16 in range(16):
                slot, rel = x_ring.acquire()
                e_x = S.emit("sp", lambda e, slot=slot, t16=t16: e.dma_start(
                    out=x_st[slot][:], in_=xr[t16 * 128:(t16 + 1) * 128, :]),
                    after=[rel], dma=f"x{slot}")
                sslot, srel = s_ring.acquire()
                e_s = S.emit("sp", lambda e, sslot=sslot, t16=t16: e.dma_start(
                    out=s_stg[sslot][:], in_=s_scr[t16 * 128:(t16 + 1) * 128, :]),
                    after=[srel] + e_spill_tot + e_expert_done, dma=f"s{sslot}")
                gslot, grel = prod_ring.acquire()
                e_g = S.emit("act", lambda e, gslot=gslot, t16=t16: e.activation(
                    prod[gslot][:], acc[:, t16 * 1024:t16 * 1024 + 1024], AF.Silu),
                    after=[acc_ev[t16], grel] + e_expert_done, ms=True)
                e_sg = S.emit("dve", lambda e, gslot=gslot, sslot=sslot: e.scalar_tensor_tensor(
                    prod[gslot][:], s_stg[sslot][:], consts[:, 6:7], prod[gslot][:],
                    ALU.mult, ALU.mult), after=[e_g, e_s, e_c], ms=True)
                e_o = S.emit("dve", lambda e, slot=slot, gslot=gslot: e.tensor_add(
                    x_st[slot][:], x_st[slot][:], prod[gslot][:]), after=[e_sg, e_x], ms=True)
                prod_ring.release(gslot, e_o)
                e_od = S.emit("sp", lambda e, t16=t16, slot=slot: e.dma_start(
                    out=out_d[t16 * 128:(t16 + 1) * 128, :], in_=x_st[slot][:]),
                    after=[e_o], dma=f"o{slot}")
                x_ring.release(slot, e_od)
                s_ring.release(sslot, e_od)
                out_evs.append(e_od)

            S.emit("sp", lambda e: e.nop(),
                   after=[S.dma_event("o0"), S.dma_event("o1"), S.dma_event("s0"),
                          S.dma_event("s1"), S.dma_event("x0"), S.dma_event("x1")])


        except _TruncEx:
            pass

        with nc.Block() as block:
            @block.tensor
            def _(e): S.run("pe", e)

            @block.scalar
            def _(e): S.run("act", e)

            @block.vector
            def _(e): S.run("dve", e)

            @block.gpsimd
            def _(e): S.run("pool", e)

            @block.sync
            def _(e): S.run("sp", e)

    return nc


_prog_cache = {}
TRUNC = None
XNOPS = 0


def kernel(L0_attn_x, L0_attn_phase, L0_attn_Wq, L0_attn_bq, L0_attn_Wk, L0_attn_bk,
           L0_attn_Wv, L0_attn_bv, L0_attn_Wo, L0_attn_bo, L0_ffn_W1, L0_ffn_W2, L0_ffn_Wr):
    x = np.ascontiguousarray(np.asarray(L0_attn_x, dtype=np.float32))
    assert x.shape == (2, 8192, 1024)

    def panels(W):
        return np.ascontiguousarray(
            np.asarray(W, np.float32).reshape(8, 128, 8, 512).transpose(2, 1, 0, 3))

    wq_p = panels(L0_attn_Wq)
    wk_p = panels(L0_attn_Wk)
    wv_p = panels(L0_attn_Wv)
    wo_h = np.ascontiguousarray(np.asarray(L0_attn_Wo, np.float32))
    w1t = np.ascontiguousarray(np.asarray(L0_ffn_W1, np.float32).transpose(0, 2, 1))
    w2t = np.ascontiguousarray(np.asarray(L0_ffn_W2, np.float32).transpose(0, 2, 1))
    wrt_h = np.ascontiguousarray(
        np.asarray(L0_ffn_Wr, np.float32).T.reshape(8, 128, 8).transpose(1, 0, 2).reshape(128, 64))
    cols = lambda v: np.ascontiguousarray(np.asarray(v, np.float32).reshape(32, 128).T)
    phct_h = cols(np.asarray(L0_attn_phase, np.float32).reshape(4096))
    bqc_h = cols(L0_attn_bq)
    bkc_h = cols(L0_attn_bk)
    bvc_h = cols(L0_attn_bv)
    bo_h = np.ascontiguousarray(np.asarray(L0_attn_bo, np.float32).reshape(1, 1024))
    sel_h = np.zeros((128, 256), dtype=ml_dtypes.bfloat16)
    for p in range(16):
        sel_h[:, p * 16 + p] = 1
    eyeb_h = np.eye(128, dtype=ml_dtypes.bfloat16)
    eyef_h = np.eye(128, dtype=np.float32)
    ones_h = np.ones((1, 128), dtype=ml_dtypes.bfloat16)

    if "nc" not in _prog_cache:
        _prog_cache["nc"] = build_program()
    nc = _prog_cache["nc"]

    in_maps = []
    for core in range(8):
        b, c2 = core // 4, core % 4
        xf = np.ascontiguousarray(x[b, 2048 * c2: 2048 * (c2 + 1), :])
        strips = [x[b, h * 2048 + 512 * c2: h * 2048 + 512 * (c2 + 1), :] for h in range(4)]
        xrr = np.ascontiguousarray(np.concatenate(strips, axis=0))
        in_maps.append({
            "xf": xf, "xr": xrr, "wq_p": wq_p, "wk_p": wk_p, "wv_p": wv_p,
            "wo_h": wo_h, "w1t": w1t, "w2t": w2t, "wrt_h": wrt_h, "phct_h": phct_h,
            "bqc_h": bqc_h, "bkc_h": bkc_h, "bvc_h": bvc_h, "bo_h": bo_h,
            "sel_h": sel_h, "eyeb_h": eyeb_h, "eyef_h": eyef_h, "ones_h": ones_h,
        })

    res = run_bass_kernel_spmd(nc, in_maps, core_ids=list(range(8)))
    out = np.empty((2, 8192, 1024), dtype=np.float32)
    for core in range(8):
        b, c2 = core // 4, core % 4
        r = np.asarray(res.results[core]["out"]).reshape(4, 512, 1024)
        for h in range(4):
            out[b, h * 2048 + 512 * c2: h * 2048 + 512 * (c2 + 1), :] = r[h]
    return out


# version 12
# speedup vs baseline: 1.0227x; 1.0227x over previous
import contextlib
import numpy as np
import ml_dtypes
import concourse.bass as bass
import concourse.mybir as mybir
from concourse.bass_utils import run_bass_kernel_spmd

F32, BF16 = mybir.dt.float32, mybir.dt.bfloat16
AF = mybir.ActivationFunctionType
ALU = mybir.AluOpType
AX = mybir.AxisListType

D = 1024
H = 4
NG = 4
TPG = 512
NT = 4
E = 8
FQ = 1024
NQ = 4
PI2 = 1.5707963267948966


class Sched:
    ENG = ("pe", "act", "dve", "pool", "sp")

    def __init__(self, nc, ctx):
        self.nc = nc
        self.ctx = ctx
        self.ops = {e: [] for e in self.ENG}
        self.sem = {e: ctx.enter_context(nc.semaphore(name='sem_' + e)) for e in self.ENG}
        self.cnt = {e: 0 for e in self.ENG}
        self.maxwait = {e: {} for e in self.ENG}

    def _sem(self, name):
        if name not in self.sem:
            self.sem[name] = self.ctx.enter_context(
                self.nc.semaphore(name='sem_' + name))
            self.cnt[name] = 0
        return self.sem[name]

    def dma_event(self, stream):
        self._sem(stream)
        return (stream, self.cnt[stream])

    def emit(self, eng, fn, after=(), ms=False, dma=None):
        waits = []
        flat = []
        for ev in after:
            if ev is None:
                continue
            if isinstance(ev, list):
                flat += [e for e in ev if e is not None]
            else:
                flat.append(ev)
        for (pname, val) in flat:
            self._sem(pname)
            if val > self.maxwait[eng].get(pname, 0):
                waits.append((self.sem[pname], val))
                self.maxwait[eng][pname] = val
        inc = None
        if dma is not None:
            self._sem(dma)
            self.cnt[dma] += 16
            inc = (self.sem[dma], 16)
            self.ops[eng].append((fn, waits, inc))
            return (dma, self.cnt[dma])
        if ms:
            self.cnt[eng] += 1
            inc = (self.sem[eng], 1)
        self.ops[eng].append((fn, waits, inc))
        return (eng, self.cnt[eng])

    def run(self, eng, engine):
        for fn, waits, inc in self.ops[eng]:
            for (sem, val) in waits:
                engine.wait_ge(sem, val)
            inst = fn(engine)
            if inc is not None:
                inst.then_inc(inc[0], inc[1])


class Ring:
    def __init__(self, n):
        self.n = n
        self.last = [None] * n
        self.i = 0

    def acquire(self):
        s = self.i % self.n
        self.i += 1
        return s, self.last[s]

    def release(self, s, ev):
        self.last[s] = ev


class _TruncEx(Exception):
    pass


def build_program():
    def _stop(tag):
        if TRUNC is not None and TRUNC == tag:
            raise _TruncEx()
    nc = bass.Bass("TRN2", target_bir_lowering=False)

    xf = nc.dram_tensor("xf", [2048, D], F32, kind="ExternalInput").ap()
    xr = nc.dram_tensor("xr", [2048, D], F32, kind="ExternalInput").ap()
    wq_p = nc.dram_tensor("wq_p", [8, 128, 8, 512], F32, kind="ExternalInput").ap()
    wk_p = nc.dram_tensor("wk_p", [8, 128, 8, 512], F32, kind="ExternalInput").ap()
    wv_p = nc.dram_tensor("wv_p", [8, 128, 8, 512], F32, kind="ExternalInput").ap()
    wo_h = nc.dram_tensor("wo_h", [4096, D], F32, kind="ExternalInput").ap()
    w1t = nc.dram_tensor("w1t", [E, D, 4096], F32, kind="ExternalInput").ap()
    w2t = nc.dram_tensor("w2t", [E, 4096, D], F32, kind="ExternalInput").ap()
    wrt_h = nc.dram_tensor("wrt_h", [128, 64], F32, kind="ExternalInput").ap()
    phct_h = nc.dram_tensor("phct_h", [128, 32], F32, kind="ExternalInput").ap()
    bqc_h = nc.dram_tensor("bqc_h", [128, 32], F32, kind="ExternalInput").ap()
    bkc_h = nc.dram_tensor("bkc_h", [128, 32], F32, kind="ExternalInput").ap()
    bvc_h = nc.dram_tensor("bvc_h", [128, 32], F32, kind="ExternalInput").ap()
    bo_h = nc.dram_tensor("bo_h", [1, D], F32, kind="ExternalInput").ap()
    sel_h = nc.dram_tensor("sel_h", [128, 256], BF16, kind="ExternalInput").ap()
    eyeb_h = nc.dram_tensor("eyeb_h", [128, 128], BF16, kind="ExternalInput").ap()
    eyef_h = nc.dram_tensor("eyef_h", [128, 128], F32, kind="ExternalInput").ap()
    ones_h = nc.dram_tensor("ones_h", [1, 128], BF16, kind="ExternalInput").ap()
    s_scr = nc.dram_tensor("s_scr", [2048, D], BF16, kind="Internal").ap()
    out_d = nc.dram_tensor("out", [2048, D], F32, kind="ExternalOutput").ap()

    ctx = contextlib.ExitStack()
    with ctx:
        sb = lambda name, shape, dt: ctx.enter_context(nc.sbuf_tensor(name, shape, dt))
        psa = lambda name, shape, dt: ctx.enter_context(nc.psum_tensor(name, shape, dt))
        S = Sched(nc, ctx)

        AB = sb("AB", [128, 40960], BF16)
        AFr = sb("AFr", [128, 16384], F32)

        x_st = [sb(f"x_st{i}", [128, D], F32) for i in range(2)]
        xn_bf = []
        xnT_f = sb("xnT_f", [128, 4096], BF16)
        wpan = [sb(f"wpan{i}", [128, 4096], BF16) for i in range(2)]
        wor = [sb(f"wor{i}", [128, 2048], BF16) for i in range(2)]
        prod = [sb(f"prod{i}", [128, 1024], BF16) for i in range(2)]
        s_stg = [sb(f"s_stg{i}", [128, D], BF16) for i in range(2)]
        vt_st = [sb(f"vt_st{i}", [128, 512], BF16) for i in range(2)]
        stats = sb("stats", [128, 64], F32)
        consts = sb("consts", [128, 8], F32)
        phcc = sb("phcc", [128, 32], F32)
        psct = sb("psct", [128, 32], F32)
        bqc = sb("bqc", [128, 32], F32)
        bkc = sb("bkc", [128, 32], F32)
        bvc = sb("bvc", [128, 32], F32)
        wcl = sb("wcl", [128, 128], F32)
        wtmp = sb("wtmp", [128, 8], F32)
        sco = sb("sco", [16, 512], F32)
        lgt = sco[0:8, :]
        atn = sb("atn", [128, 64], F32)
        smx = sb("smx", [128, 16], F32)
        zs = sb("zs", [128, 16], F32)
        zr = sb("zr", [128, 16], F32)
        lgtm = sb("lgtm", [128, 32], F32)
        gate = sb("gate", [128, 128], F32)
        wrt_s = sb("wrt_s", [128, 64], BF16)
        bo_bf = sb("bo_bf", [1, D], BF16)
        sel_s = sb("sel_s", [128, 256], BF16)
        eyeb = sb("eyeb", [128, 128], BF16)
        eyef = sb("eyef", [128, 128], F32)
        ones_r = sb("ones_r", [1, 128], BF16)
        gct = sb("gct", [128, 16], F32)

        PD = [psa(f"PD{i}", [128, 1024], F32) for i in range(3)]
        PB3 = psa("PB3", [128, 2048], BF16)
        bank_ev = {}

        def bank_guard(key):
            return bank_ev.get(key, [])

        ld = []
        ld.append(S.emit("pool", lambda e: e.dma_start(out=wrt_s[:], in_=wrt_h[:, :]), dma="miscp"))
        ld.append(S.emit("sp", lambda e: e.dma_start(out=phcc[:], in_=phct_h[:, :]), dma="misc"))
        ld.append(S.emit("sp", lambda e: e.dma_start(out=bqc[:], in_=bqc_h[:, :]), dma="misc"))
        ld.append(S.emit("sp", lambda e: e.dma_start(out=bkc[:], in_=bkc_h[:, :]), dma="misc"))
        ld.append(S.emit("sp", lambda e: e.dma_start(out=bvc[:], in_=bvc_h[:, :]), dma="misc"))
        ld.append(S.emit("sp", lambda e: e.dma_start(out=sel_s[:], in_=sel_h[:, :]), dma="misc"))
        ld.append(S.emit("sp", lambda e: e.dma_start(out=eyeb[:], in_=eyeb_h[:, :]), dma="misc"))
        ld.append(S.emit("sp", lambda e: e.dma_start(out=eyef[:], in_=eyef_h[:, :]), dma="misc"))
        ld.append(S.emit("sp", lambda e: e.dma_start(out=ones_r[:], in_=ones_h[:, :]), dma="misc"))
        e_bo = S.emit("pool", lambda e: e.dma_start(out=bo_bf[:], in_=bo_h[:, :]), dma="miscp")
        e_misc = [S.dma_event("misc"), S.dma_event("miscp")]

        cvals = [PI2, -1.0 / D, 1.0 / D, 1e-5, -0.5, 1.0 / 32.0, 1.0, 0.0]
        e_c = None
        for i, v in enumerate(cvals):
            e_c = S.emit("dve", lambda e, i=i, v=v: e.memset(consts[:, i:i + 1], float(v)),
                         after=[e_c] if e_c else [], ms=True)

        e_sin = S.emit("act", lambda e: e.activation(psct[:], phcc[:], AF.Sin),
                       after=[e_misc, e_c], ms=True)
        e_cos = S.emit("act", lambda e: e.activation(phcc[:], phcc[:], AF.Sin, bias=consts[:, 0:1]),
                       after=[e_sin], ms=True)
        e_w = e_cos
        for pr in range(16):
            hh, gg2 = pr // 4, pr % 4
            e_w1_ = S.emit("dve", lambda e, hh=hh, gg2=gg2, pr=pr: e.tensor_mul(
                wcl[:, pr * 8:pr * 8 + 8], phcc[:, hh * 8:hh * 8 + 8],
                phcc[:, gg2 * 8:gg2 * 8 + 8]), after=[e_w], ms=True)
            e_w2_ = S.emit("dve", lambda e, hh=hh, gg2=gg2: e.tensor_mul(
                wtmp[:], psct[:, hh * 8:hh * 8 + 8], psct[:, gg2 * 8:gg2 * 8 + 8]),
                after=[e_w1_], ms=True)
            e_w = S.emit("dve", lambda e, pr=pr: e.tensor_add(
                wcl[:, pr * 8:pr * 8 + 8], wcl[:, pr * 8:pr * 8 + 8], wtmp[:]),
                after=[e_w2_], ms=True)
        e_trig = [e_w]

        ln_chain = [None]
        ln_slot_ev = {}
        ln_slot_ctr = [0]
        ln_sq_ev = [None]

        def emit_ln(x_ev, x_slot, xn_slot, extra_after=()):
            sl = ln_slot_ctr[0] % 8
            ln_slot_ctr[0] += 1
            c0 = sl * 8
            after0 = [x_ev, e_c, ln_slot_ev.get(sl)] + list(extra_after)
            e1 = S.emit("dve", lambda e: e.reduce_sum(stats[:, c0:c0 + 1], x_st[x_slot][:],
                                                      AX.X), after=after0, ms=True)
            e2 = S.emit("act", lambda e: e.activation(stats[:, c0 + 1:c0 + 2],
                                                      stats[:, c0:c0 + 1], AF.Identity,
                                                      scale=consts[:, 1:2]), after=[e1], ms=True)
            e3 = S.emit("dve", lambda e: e.tensor_scalar_add(x_st[x_slot][:], x_st[x_slot][:],
                                                             stats[:, c0 + 1:c0 + 2]),
                        after=[e2], ms=True)
            e4 = S.emit("act", lambda e: e.activation(sqd, x_st[x_slot][:], AF.Square,
                                                      accum_out=stats[:, c0 + 2:c0 + 3]),
                        after=[e3, ln_sq_ev[0]], ms=True)
            ln_sq_ev[0] = e4
            e5 = S.emit("act", lambda e: e.activation(stats[:, c0 + 3:c0 + 4],
                                                      stats[:, c0 + 2:c0 + 3], AF.Ln,
                                                      scale=consts[:, 2:3], bias=consts[:, 3:4]),
                        after=[e4], ms=True)
            e6 = S.emit("act", lambda e: e.activation(stats[:, c0 + 4:c0 + 5],
                                                      stats[:, c0 + 3:c0 + 4], AF.Exp,
                                                      scale=consts[:, 4:5]), after=[e5], ms=True)
            e7 = S.emit("dve", lambda e: e.tensor_scalar_mul(xn_bf[xn_slot][:],
                                                             x_st[x_slot][:],
                                                             stats[:, c0 + 4:c0 + 5]),
                        after=[e6], ms=True)
            ln_slot_ev[sl] = e7
            ln_chain[0] = e7
            return e7

        def emit_xn_transpose(xn_ev, xn_slot, dstT_ap, cstride, dst_col, extra_after=()):
            evs = []
            for cc in range(2):
                half = cc
                key = ("PD2", half)
                aft = [xn_ev] + list(extra_after) + bank_guard(key)
                last = None
                for c in range(4 * cc, 4 * cc + 4):
                    off = half * 512 + (c % 4) * 128
                    last = S.emit("pe", lambda e, c=c, off=off: e.transpose(
                        PD[2][:, off:off + 128], xn_bf[xn_slot][:, c * 128:(c + 1) * 128],
                        eyef[:]), after=aft, ms=(c == 4 * cc + 3))
                    aft = []
                ev = S.emit("act", lambda e, cc=cc, half=half: e.copy(
                    dstT_ap.rearrange("p (c t) -> p c t", c=8 * (4096 // cstride) // 8 if False
                                      else (4096 * 8 // (cstride * 8)) if False else 8)
                    [:, 4 * cc:4 * cc + 4, dst_col:dst_col + 128]
                    if cstride * 8 == dstT_ap.shape[-1] else None,
                    PD[2][:, half * 512:half * 512 + 512].rearrange("p (c t) -> p c t", c=4)),
                    after=[last], ms=True)
                bank_ev[key] = [ev]
                evs.append(ev)
            return evs[-1]

        def emit_xn_transpose2(xn_ev, xn_slot, dstT_ap, cstride, dst_col, extra_after=()):
            evs = []
            dst3 = dstT_ap.rearrange("p (c t) -> p c t", t=cstride)
            for cc in range(2):
                half = cc
                key = ("PD2", half)
                aft = [xn_ev] + list(extra_after) + bank_guard(key)
                last = None
                for c in range(4 * cc, 4 * cc + 4):
                    off = half * 512 + (c % 4) * 128
                    last = S.emit("pe", lambda e, c=c, off=off: e.transpose(
                        PD[2][:, off:off + 128], xn_bf[xn_slot][:, c * 128:(c + 1) * 128],
                        eyef[:]), after=aft, ms=(c == 4 * cc + 3))
                    aft = []
                ev = S.emit("act", lambda e, cc=cc, half=half: e.copy(
                    dst3[:, 4 * cc:4 * cc + 4, dst_col:dst_col + 128],
                    PD[2][:, half * 512:half * 512 + 512].rearrange("p (c t) -> p c t", c=4)),
                    after=[last], ms=True)
                bank_ev[key] = [ev]
                evs.append(ev)
            return evs[-1]

        def proj_chunk(nc_idx, pan_slot, pan_ev, xnT_ap, xnT_ev, pp):
            key = ("PD0", pp)
            aft = [pan_ev, xnT_ev] + bank_guard(key)
            last = None
            for dc in range(8):
                lhs = wpan[pan_slot][:, dc * 512 + (nc_idx % 4) * 128:
                                     dc * 512 + (nc_idx % 4) * 128 + 128]
                last = S.emit("pe", lambda e, lhs=lhs, dc=dc, pp=pp: e.matmul(
                    PD[0][:, pp * 512:pp * 512 + 512], lhs,
                    xnT_ap[:, dc * 512:dc * 512 + 512],
                    start=(dc == 0), stop=(dc == 7)),
                    after=aft if dc == 0 else [], ms=(dc == 7))
                aft = []
            return last

        try:
            kbT = AB[:, 0:16384]
            qbT = AB[:, 32768:36864]
            qw_sl = [AB[:, 36864:37888], AB[:, 37888:38912]]
            v_tm = AB[:, 0:16384]
            outT = AB[:, 16384:32768]
            out_tm = [AFr[:, 0:4096], AFr[:, 4096:8192]]
            out_bf = [AB[:, 32768:36864], AB[:, 36864:40960]]
            sqd = AFr[:, 8192:9216]
            xn_bf.append(AFr[:, 9216:10240])
            xn_bf.append(AFr[:, 10240:11264])

            pan_ring = Ring(2)
            prod_sl = [prod[0][:, 0:512], prod[0][:, 512:1024],
                       prod[1][:, 0:512], prod[1][:, 512:1024]]
            prod_ev = {i: None for i in range(4)}
            wo_ring = Ring(2)
            prod_ring = Ring(2)
            x_ring = Ring(2)
            xn_ring = Ring(2)
            vt_ring = Ring(2)
            s_ring = Ring(2)

            e_spill_all = []
            group_wo_done = None
            e_xnT_ffn_evs = []

            sc_state = {"first": True, "aft0": None, "last": None}

            def emit_front_ln(g):
                last = None
                for tt in range(NT):
                    slot, rel = x_ring.acquire()
                    e_x = S.emit("sp", lambda e, slot=slot, g=g, tt=tt: e.dma_start(
                        out=x_st[slot][:], in_=xf[g * TPG + tt * 128: g * TPG + (tt + 1) * 128, :]),
                        after=[rel], dma=f"x{slot}")
                    xnslot, xnrel = xn_ring.acquire()
                    e_ln = emit_ln(e_x, slot, xnslot, extra_after=[xnrel] + e_trig)
                    x_ring.release(slot, e_ln)
                    e_tr = emit_xn_transpose2(e_ln, xnslot, xnT_f[:], 512, tt * 128)
                    xn_ring.release(xnslot, e_tr)
                    last = e_tr
                return last

            for g in range(NG):
                e_xnT = emit_front_ln(g)
                _stop(f"g{g}_ln")

                k_ev_last = []
                for pan in range(8):
                    pslot, prel = pan_ring.acquire()
                    e_pan = S.emit("pool", lambda e, pslot=pslot, pan=pan: e.dma_start(
                        out=wpan[pslot][:].rearrange("p (c n) -> p c n", c=8), in_=wk_p[pan]),
                        after=[prel], dma=f"pan{pslot}")
                    lastmm = None
                    for sub in range(4):
                        nc_idx = pan * 4 + sub
                        pp = nc_idx % 2
                        e_mm = proj_chunk(nc_idx, pslot, e_pan, xnT_f[:], e_xnT, pp)
                        if nc_idx % 2 == 0:
                            e_v1 = S.emit("dve", lambda e, nc_idx=nc_idx, pp=pp:
                                          e.tensor_scalar_add(
                                kbT[:, nc_idx * 512:nc_idx * 512 + 512],
                                PD[0][:, pp * 512:pp * 512 + 512],
                                bkc[:, nc_idx:nc_idx + 1]), after=[e_mm], ms=True)
                        else:
                            e_v1 = S.emit("act", lambda e, nc_idx=nc_idx, pp=pp: e.activation(
                                kbT[:, nc_idx * 512:nc_idx * 512 + 512],
                                PD[0][:, pp * 512:pp * 512 + 512], AF.Identity,
                                bias=bkc[:, nc_idx:nc_idx + 1]), after=[e_mm], ms=True)
                        bank_ev[("PD0", pp)] = [e_v1]
                        lastmm = e_mm
                        k_ev_last = [e_v1]
                    pan_ring.release(pslot, lastmm)
                _stop(f"g{g}_k")

                for h in range(H):
                    q_ev_last = []
                    lastmm = {}
                    pans = {}
                    for hp in range(2):
                        pslot, prel = pan_ring.acquire()
                        e_pan = S.emit("pool", lambda e, pslot=pslot, h=h, hp=hp: e.dma_start(
                            out=wpan[pslot][:].rearrange("p (c n) -> p c n", c=8),
                            in_=wq_p[2 * h + hp]), after=[prel], dma=f"pan{pslot}")
                        pans[hp] = (pslot, e_pan)
                    e_pan_all = [S.dma_event("pan0"), S.dma_event("pan1")]
                    q_ev_byc = {}
                    for sub in range(8):
                        nc_idx = h * 8 + sub
                        pslot, e_pan = pans[sub // 4]
                        pp = nc_idx % 2
                        e_mm = proj_chunk(nc_idx, pslot, e_pan_all, xnT_f[:], e_xnT, pp)
                        lastmm[sub // 4] = e_mm
                        e_q2 = S.emit("act", lambda e, nc_idx=nc_idx, sub=sub, pp=pp: e.activation(
                            qbT[:, sub * 512:sub * 512 + 512],
                            PD[0][:, pp * 512:pp * 512 + 512], AF.Identity,
                            bias=bqc[:, nc_idx:nc_idx + 1]), after=[e_mm], ms=True)
                        bank_ev[("PD0", pp)] = [e_q2]
                        q_ev_byc[sub] = e_q2
                        q_ev_last = [e_q2]
                    pan_ring.release(pans[0][0], lastmm[0])
                    pan_ring.release(pans[1][0], lastmm[1])
                    for gg in range(H):
                        pair = h * 4 + gg
                        for ck in range(8):
                            prslot = ck % 4
                            prrel = prod_ev[prslot]
                            e_pr = S.emit("dve", lambda e, prslot=prslot, ck=ck, gg=gg,
                                          pair=pair: e.scalar_tensor_tensor(
                                prod_sl[prslot],
                                qbT[:, ck * 512:ck * 512 + 512],
                                wcl[:, pair * 8 + ck:pair * 8 + ck + 1],
                                kbT[:, (gg * 8 + ck) * 512:(gg * 8 + ck) * 512 + 512],
                                ALU.mult, ALU.mult),
                                after=[q_ev_byc[ck], k_ev_last, prrel], ms=True)
                            aftm = [e_pr, e_misc]
                            if sc_state["first"] and sc_state["aft0"] is not None:
                                aftm.append(sc_state["aft0"])
                            is_last = (pair == 15 and ck == 7)
                            e_smm = S.emit("pe", lambda e, prslot=prslot, pair=pair,
                                           st=sc_state["first"], sp_=is_last: e.matmul(
                                PD[1][0:16, 0:512],
                                sel_s[:, pair * 16:pair * 16 + 16],
                                prod_sl[prslot],
                                start=st, stop=sp_),
                                after=aftm, ms=True)
                            sc_state["first"] = False
                            prod_ev[prslot] = e_smm
                            sc_state["last"] = e_smm

                _stop(f"g{g}_q")
                e_sc = S.emit("dve", lambda e: e.tensor_scalar_mul(sco[:], PD[1][0:16, 0:512],
                                                                   consts[0:16, 5:6]),
                              after=[sc_state["last"], e_c], ms=True)
                sc_state["first"] = True
                sc_state["aft0"] = e_sc
                atn_evs = []
                for tt in range(NT):
                    key = ("PD1T",)
                    aft = [e_sc] + bank_guard(key)
                    e_t = S.emit("pe", lambda e, tt=tt: e.transpose(
                        PD[1][:, 512 + tt * 16:512 + tt * 16 + 16],
                        sco[:, tt * 128:tt * 128 + 128], eyef[0:16, 0:16]), after=aft, ms=True)
                    e_a = S.emit("dve", lambda e, tt=tt: e.tensor_copy(
                        atn[:, tt * 16:tt * 16 + 16], PD[1][:, 512 + tt * 16:512 + tt * 16 + 16]),
                        after=[e_t], ms=True)
                    bank_ev[key] = [e_a]
                    atn_evs.append(e_a)
                smax_evs = []
                for tt in range(NT):
                    for h in range(H):
                        c0 = tt * 16 + h * 4
                        e_m = S.emit("dve", lambda e, c0=c0, tt=tt, h=h: e.reduce_max(
                            smx[:, tt * 4 + h:tt * 4 + h + 1], atn[:, c0:c0 + 4], AX.X, negate=True),
                            after=[atn_evs[tt]], ms=True)
                        e_e = S.emit("act", lambda e, c0=c0, tt=tt, h=h: e.activation(
                            atn[:, c0:c0 + 4], atn[:, c0:c0 + 4], AF.Exp,
                            bias=smx[:, tt * 4 + h:tt * 4 + h + 1],
                            accum_out=zs[:, tt * 4 + h:tt * 4 + h + 1]), after=[e_m], ms=True)
                        smax_evs.append(e_e)
                e_zr = S.emit("dve", lambda e: e.reciprocal(zr[:], zs[:]), after=smax_evs, ms=True)
                e_attn = None
                for tt in range(NT):
                    for h in range(H):
                        c0 = tt * 16 + h * 4
                        e_attn = S.emit("dve", lambda e, c0=c0, tt=tt, h=h: e.tensor_scalar_mul(
                            atn[:, c0:c0 + 4], atn[:, c0:c0 + 4],
                            zr[:, tt * 4 + h:tt * 4 + h + 1]), after=[e_zr], ms=True)

                _stop(f"g{g}_smax")
                v_last = None
                for pan in range(8):
                    pslot, prel = pan_ring.acquire()
                    e_pan = S.emit("pool", lambda e, pslot=pslot, pan=pan: e.dma_start(
                        out=wpan[pslot][:].rearrange("p (c n) -> p c n", c=8), in_=wv_p[pan]),
                        after=[prel], dma=f"pan{pslot}")
                    lastmm = None
                    for sub in range(4):
                        nc_idx = pan * 4 + sub
                        pp = nc_idx % 2
                        e_mm = proj_chunk(nc_idx, pslot, e_pan, xnT_f[:], e_xnT, pp)
                        lastmm = e_mm
                        vslot, vrel = vt_ring.acquire()
                        e_ve = S.emit("act", lambda e, nc_idx=nc_idx, pp=pp, vslot=vslot: e.activation(
                            vt_st[vslot][:], PD[0][:, pp * 512:pp * 512 + 512], AF.Identity,
                            bias=bvc[:, nc_idx:nc_idx + 1]), after=[e_mm, vrel], ms=True)
                        bank_ev[("PD0", pp)] = [e_ve]
                        half = nc_idx % 2
                        keyt = ("PB3", half)
                        aftt = [e_ve] + bank_guard(keyt)
                        last = None
                        for b4 in range(4):
                            last = S.emit("pe", lambda e, b4=b4, half=half, vslot=vslot: e.transpose(
                                PB3[:, half * 1024 + b4 * 128:half * 1024 + b4 * 128 + 128],
                                vt_st[vslot][:, b4 * 128:b4 * 128 + 128], eyeb[:]),
                                after=aftt, ms=(b4 == 3))
                            aftt = []
                        vt_ring.release(vslot, last)
                        e_vt = S.emit("dve", lambda e, nc_idx=nc_idx, half=half: e.tensor_copy(
                            v_tm.rearrange("p (t n) -> p t n", t=4)[:, :,
                                                                    nc_idx * 128:nc_idx * 128 + 128],
                            PB3[:, half * 1024:half * 1024 + 512].rearrange("p (t n) -> p t n", t=4)),
                            after=[last], ms=True)
                        bank_ev[keyt] = [e_vt]
                        v_last = e_vt
                    pan_ring.release(pslot, lastmm)

                _stop(f"g{g}_v")
                out_ev_tiles = []
                for tt in range(NT):
                    obf = out_bf[tt % 2]
                    prev_ot = out_ev_tiles[tt - 2] if tt >= 2 else None
                    ao_evs = []
                    for h in range(H):
                        e_ao = S.emit("dve", lambda e, tt=tt, h=h, obf=obf: e.tensor_scalar_mul(
                            obf[:, h * 1024:h * 1024 + 1024],
                            v_tm[:, tt * 4096:tt * 4096 + 1024],
                            atn[:, tt * 16 + h * 4:tt * 16 + h * 4 + 1]),
                            after=[v_last, e_attn, prev_ot], ms=True)
                        for gg in range(1, 4):
                            e_ao = S.emit("dve", lambda e, tt=tt, h=h, gg=gg, obf=obf:
                                          e.scalar_tensor_tensor(
                                obf[:, h * 1024:h * 1024 + 1024],
                                v_tm[:, tt * 4096 + gg * 1024:tt * 4096 + gg * 1024 + 1024],
                                atn[:, tt * 16 + h * 4 + gg:tt * 16 + h * 4 + gg + 1],
                                obf[:, h * 1024:h * 1024 + 1024],
                                ALU.mult, ALU.add), after=[e_ao], ms=True)
                        ao_evs.append(e_ao)
                    last_oe = None
                    for oc in range(8):
                        c0 = oc * 4
                        half = oc % 2
                        keyt = ("PB3", half)
                        aftt = [ao_evs[c0 // 8]] + bank_guard(keyt)
                        last = None
                        for b4 in range(4):
                            c = c0 + b4
                            last = S.emit("pe", lambda e, c=c, b4=b4, half=half, obf=obf:
                                          e.transpose(
                                PB3[:, half * 1024 + b4 * 128:half * 1024 + b4 * 128 + 128],
                                obf[:, c * 128:c * 128 + 128], eyeb[:]),
                                after=aftt, ms=(b4 == 3))
                            aftt = []
                        last_oe = S.emit("act", lambda e, c0=c0, half=half, tt=tt: e.copy(
                            outT.rearrange("p (c t) -> p c t", c=32)[:, c0:c0 + 4,
                                                                     tt * 128:tt * 128 + 128],
                            PB3[:, half * 1024:half * 1024 + 512]
                            .rearrange("p (c t) -> p c t", c=4)), after=[last], ms=True)
                        bank_ev[keyt] = [last_oe]
                    out_ev_tiles.append(last_oe)
                e_outT = out_ev_tiles[-1]

                _stop(f"g{g}_ao")
                aft_wo0 = [e_outT]
                for key in [("PD0", 0), ("PD0", 1), ("PD1T",), ("PD1S",), ("PD2", 0),
                            ("PD2", 1), ("PB3", 0), ("PB3", 1)]:
                    aft_wo0 += bank_guard(key)
                aft_wo0.append(sc_state["aft0"])
                wo_last = {}
                first_wo = True
                for nhf in range(2):
                    for kc4 in range(8):
                        wslot, wrel = wo_ring.acquire()
                        e_wo = S.emit("pool", lambda e, wslot=wslot, kc4=kc4, nhf=nhf:
                                      e.dma_start(
                            out=wor[wslot][:].rearrange("p (c n) -> p c n", c=4),
                            in_=wo_h[kc4 * 512:(kc4 + 1) * 512,
                                     nhf * 512:(nhf + 1) * 512]
                            .rearrange("(c p) n -> p c n", p=128)), after=[wrel],
                            dma=f"wo{wslot}")
                        lastuse = None
                        for sub in range(4):
                            kc = kc4 * 4 + sub
                            j, dr = kc // 8, kc % 8
                            for h in range(H):
                                lhs = outT[:, (h * 8 + dr) * 512 + j:
                                           (h * 8 + dr) * 512 + 512: 4]
                                aftm = [e_wo]
                                if first_wo:
                                    aftm += aft_wo0
                                    first_wo = False
                                if kc == 0 and nhf == 1:
                                    aftm.append(wo_last.get(h))
                                e_wm = S.emit("pe", lambda e, h=h, wslot=wslot, kc=kc,
                                              sub=sub, lhs=lhs: e.matmul(
                                    PD[h // 2][:, (h % 2) * 512:
                                               (h % 2) * 512 + 512], lhs,
                                    wor[wslot][:, sub * 512:sub * 512 + 512],
                                    start=(kc == 0), stop=False), after=aftm,
                                    ms=(h == 3))
                                lastuse = e_wm
                                wo_last[h] = e_wm
                        wo_ring.release(wslot, lastuse)
                    for h in range(H):
                        e_bo_mm = S.emit("pe", lambda e, h=h, nhf=nhf: e.matmul(
                            PD[h // 2][:, (h % 2) * 512:(h % 2) * 512 + 512], ones_r[:],
                            bo_bf[:, nhf * 512:nhf * 512 + 512], start=False, stop=True),
                            after=[e_misc, wo_last[h]], ms=True)
                        wo_last[h] = e_bo_mm
                        sslot, srel = s_ring.acquire()
                        e_se = S.emit("dve", lambda e, h=h, sslot=sslot: e.tensor_copy(
                            s_stg[sslot][:, 0:512],
                            PD[h // 2][:, (h % 2) * 512:(h % 2) * 512 + 512]),
                            after=[e_bo_mm, srel], ms=True)
                        e_sp = S.emit("sp", lambda e, h=h, g=g, sslot=sslot, nhf=nhf:
                                      e.dma_start(
                            out=s_scr[h * 512 + g * 128: h * 512 + (g + 1) * 128,
                                      nhf * 512:(nhf + 1) * 512],
                            in_=s_stg[sslot][:, 0:512]), after=[e_se], dma=f"s{sslot}")
                        s_ring.release(sslot, e_sp)
                        e_spill_all.append(e_sp)
                        if nhf == 1:
                            if h == 0:
                                bank_ev[("PD0", 0)] = [e_se]
                            elif h == 1:
                                bank_ev[("PD0", 1)] = [e_se]
                            elif h == 2:
                                bank_ev[("PD1S",)] = [e_se]
                            else:
                                bank_ev[("PD1T",)] = [e_se]
                        wo_last[h] = e_se
                group_wo_done = wo_last[3]
                sc_state["aft0"] = bank_ev[("PD1T",)][0]
                bank_ev[("PD2", 0)] = bank_ev.get(("PD2", 0), [])
                bank_ev[("PD2", 1)] = bank_ev.get(("PD2", 1), [])
                _stop(f"g{g}_wo")

            xnT_ffn = AB[:, 0:16384]
            hT = AB[:, 16384:20480]
            w1s = [AB[:, 20480 + i * 8192: 20480 + (i + 1) * 8192] for i in range(2)]
            w2slots = [wpan[0][:, i * 1024:(i + 1) * 1024] for i in range(4)] + \
                      [wpan[1][:, i * 1024:(i + 1) * 1024] for i in range(4)] + \
                      [xnT_f[:, i * 1024:(i + 1) * 1024] for i in range(4)] + \
                      [AB[:, 36864 + i * 2048: 36864 + (i + 1) * 2048][:, 0:1024]
                       for i in range(2)] + [prod[0][:], prod[1][:]]
            acc = AFr[:, 0:16384]

            ffn_guard = [group_wo_done, e_spill_all[-1], e_attn]

            lgt_evs = []
            e_xnT_strip = {}
            e_gate_strip = {}
            for g in range(NG):
                xnT_evs = []
                for tt in range(NT):
                    slot, rel = x_ring.acquire()
                    e_x = S.emit("sp", lambda e, slot=slot, g=g, tt=tt: e.dma_start(
                        out=x_st[slot][:], in_=xr[g * TPG + tt * 128: g * TPG + (tt + 1) * 128, :]),
                        after=[rel], dma=f"x{slot}")
                    xnslot, xnrel = xn_ring.acquire()
                    e_ln = emit_ln(e_x, slot, xnslot, extra_after=[xnrel])
                    x_ring.release(slot, e_ln)
                    e_tr = emit_xn_transpose2(e_ln, xnslot, xnT_ffn, 2048, g * 512 + tt * 128,
                                              extra_after=ffn_guard)
                    xn_ring.release(xnslot, e_tr)
                    xnT_evs.append(e_tr)
                e_xnT_ffn_evs.append(xnT_evs[-1])
                e_xnT_strip[g] = xnT_evs[-1]
                key = ("PD1S",)
                aft = [xnT_evs[-1], e_misc] + bank_guard(key)
                last = None
                for dc in range(8):
                    last = S.emit("pe", lambda e, dc=dc, g=g: e.matmul(
                        PD[1][0:8, 0:512], wrt_s[:, dc * 8:dc * 8 + 8],
                        xnT_ffn[:, dc * 2048 + g * 512:dc * 2048 + g * 512 + 512],
                        start=(dc == 0), stop=(dc == 7)),
                        after=aft if dc == 0 else [], ms=(dc == 7))
                    aft = []
                e_le = S.emit("dve", lambda e: e.tensor_copy(lgt, PD[1][0:8, 0:512]),
                              after=[last], ms=True)
                bank_ev[key] = [e_le]
                for tt in range(NT):
                    keyt = ("PD1T",)
                    aftt = [e_le] + bank_guard(keyt)
                    e_t = S.emit("pe", lambda e, tt=tt: e.transpose(
                        PD[1][:, 512 + tt * 8:512 + tt * 8 + 8],
                        lgt[:, tt * 128:tt * 128 + 128], eyef[0:8, 0:8]), after=aftt, ms=True)
                    e_lt = S.emit("dve", lambda e, tt=tt: e.tensor_copy(
                        lgtm[:, tt * 8:tt * 8 + 8], PD[1][:, 512 + tt * 8:512 + tt * 8 + 8]),
                        after=[e_t], ms=True)
                    bank_ev[keyt] = [e_lt]
                    t16 = g * 4 + tt
                    e_gm = S.emit("dve", lambda e, tt=tt: e.reduce_max(
                        gct[:, 0:1], lgtm[:, tt * 8:tt * 8 + 8], AX.X, negate=True),
                        after=[e_lt], ms=True)
                    e_ge = S.emit("act", lambda e, tt=tt, t16=t16: e.activation(
                        gate[:, t16 * 8:t16 * 8 + 8], lgtm[:, tt * 8:tt * 8 + 8], AF.Exp,
                        bias=gct[:, 0:1]), after=[e_gm], ms=True)
                    e_v1 = S.emit("dve", lambda e, t16=t16: e.reduce_max(
                        gct[:, 1:2], gate[:, t16 * 8:t16 * 8 + 8], AX.X), after=[e_ge], ms=True)
                    e_msk = S.emit("dve", lambda e, t16=t16: e.tensor_single_scalar(
                        gct[:, 4:12], gate[:, t16 * 8:t16 * 8 + 8], gct[:, 1:2], ALU.is_lt),
                        after=[e_v1], ms=True)
                    e_em = S.emit("dve", lambda e, t16=t16: e.tensor_mul(
                        gct[:, 4:12], gct[:, 4:12], gate[:, t16 * 8:t16 * 8 + 8]),
                        after=[e_msk], ms=True)
                    e_v2 = S.emit("dve", lambda e: e.reduce_max(gct[:, 2:3], gct[:, 4:12], AX.X),
                                  after=[e_em], ms=True)
                    e_dn = S.emit("dve", lambda e: e.tensor_add(gct[:, 3:4], gct[:, 1:2],
                                                                gct[:, 2:3]), after=[e_v2], ms=True)
                    e_rd = S.emit("dve", lambda e: e.reciprocal(gct[:, 3:4], gct[:, 3:4]),
                                  after=[e_dn], ms=True)
                    e_gmask = S.emit("dve", lambda e, t16=t16: e.tensor_single_scalar(
                        gct[:, 4:12], gate[:, t16 * 8:t16 * 8 + 8], gct[:, 2:3], ALU.is_ge),
                        after=[e_rd], ms=True)
                    e_gsc = S.emit("dve", lambda e, t16=t16: e.tensor_scalar_mul(
                        gate[:, t16 * 8:t16 * 8 + 8], gate[:, t16 * 8:t16 * 8 + 8], gct[:, 3:4]),
                        after=[e_gmask], ms=True)
                    e_gf = S.emit("dve", lambda e, t16=t16: e.tensor_mul(
                        gate[:, t16 * 8:t16 * 8 + 8], gate[:, t16 * 8:t16 * 8 + 8], gct[:, 4:12]),
                        after=[e_gsc], ms=True)
                    lgt_evs.append(e_gf)
                    e_gate_strip[g] = e_gf
            e_gate = lgt_evs[-1]
            _stop("ffnln")
            e_xnT_all = e_xnT_ffn_evs[-1]

            acc_ev = {}
            for g in range(NG):
                aft_m = ffn_guard + [e_xnT_strip[g]]
                if g == 2:
                    aft_m += [ln_chain[0], e_xnT_all]
                e_acc0 = S.emit("dve", lambda e, g=g: e.memset(
                    acc[:, g * 4096:(g + 1) * 4096], 0.0), after=aft_m, ms=True)
                for ts in range(4):
                    acc_ev[g * 4 + ts] = e_acc0

            w1_ring = Ring(2)
            w2_ring = Ring(2)
            ybank = {0: bank_guard(("PD1S",)) + bank_guard(("PD1T",)),
                     1: bank_guard(("PD2", 0)) + bank_guard(("PD2", 1))}
            yrot = [0]
            for ex in range(E):
                for qq in range(NQ):
                    piece = ex * NQ + qq
                    pslot, prel = w1_ring.acquire()
                    w2par, w2rel = w2_ring.acquire()
                    w2evs = []
                    for fc in range(8):
                        aft2 = []
                        if fc == 0:
                            aft2 = [w2rel]
                            if piece == 0:
                                aft2 += ffn_guard
                        e_w2c = S.emit("pool", lambda e, w2par=w2par, ex=ex, qq=qq, fc=fc:
                                       e.dma_start(
                            out=w2slots[w2par * 8 + fc],
                            in_=w2t[ex, qq * FQ + fc * 128:qq * FQ + (fc + 1) * 128, :]),
                            after=aft2, dma=f"w2{w2par}")
                        w2evs.append(e_w2c)
                    e_w2 = w2evs[-1]
                    w1evs = []
                    for dc in range(8):
                        aft = []
                        if dc == 0:
                            aft = [prel]
                            if piece == 0:
                                aft += ffn_guard
                        e_w1 = S.emit("pool", lambda e, pslot=pslot, ex=ex, qq=qq, dc=dc: e.dma_start(
                            out=w1s[pslot][:, dc * 1024:(dc + 1) * 1024],
                            in_=w1t[ex, dc * 128:(dc + 1) * 128, qq * FQ:(qq + 1) * FQ]),
                            after=aft, dma=f"w1{pslot}")
                        w1evs.append(e_w1)
                    lastmm_piece = None
                    last_y_mm = [None]
                    for g in range(NG):
                        sil_evs = []
                        for fc in range(8):
                            pp = fc % 2
                            key = ("PD0", pp)
                            aft = [w1evs[-1], e_xnT_strip[g]] + bank_guard(key)
                            last = None
                            for dc in range(8):
                                lhs = w1s[pslot][:, dc * 1024 + fc * 128:dc * 1024 + fc * 128 + 128]
                                last = S.emit("pe", lambda e, lhs=lhs, dc=dc, pp=pp, g=g: e.matmul(
                                    PD[0][:, pp * 512:pp * 512 + 512], lhs,
                                    xnT_ffn[:, dc * 2048 + g * 512:dc * 2048 + g * 512 + 512],
                                    start=(dc == 0), stop=(dc == 7)),
                                    after=aft if dc == 0 else [], ms=(dc == 7))
                                aft = []
                            lastmm_piece = last
                            e_sil = S.emit("act", lambda e, fc=fc, pp=pp: e.activation(
                                hT[:, fc * 512:fc * 512 + 512], PD[0][:, pp * 512:pp * 512 + 512],
                                AF.Silu), after=[last], ms=True)
                            bank_ev[key] = [e_sil]
                            sil_evs.append(e_sil)
                        for pas in range(2):
                            ts_pair = (0, 1) if pas == 0 else (2, 3)
                            tsbank = {}
                            for ts in ts_pair:
                                tsbank[ts] = 1 + (yrot[0] % 2)
                                yrot[0] += 1
                            y_last = {}
                            for ts in ts_pair:
                                bi = tsbank[ts]
                                first = True
                                for fc in range(8):
                                    lhs = hT[:, fc * 512 + ts * 128:fc * 512 + ts * 128 + 128]
                                    for nh in range(2):
                                        aftm = [sil_evs[fc], e_w2]
                                        if first:
                                            aftm += ybank[bi - 1]
                                        e_ym = S.emit("pe", lambda e, bi=bi, nh=nh,
                                                      w2par=w2par, fc=fc, lhs=lhs,
                                                      st=(fc == 0),
                                                      sp_=(fc == 7): e.matmul(
                                            PD[bi][:, nh * 512:nh * 512 + 512], lhs,
                                            w2slots[w2par * 8 + fc][:, nh * 512:nh * 512 + 512],
                                            start=st, stop=sp_), after=aftm, ms=(nh == 1))
                                        first = False
                                    y_last[ts] = e_ym
                                    last_y_mm[0] = e_ym
                            for ts in ts_pair:
                                bi = tsbank[ts]
                                t16 = g * 4 + ts
                                e_ya = S.emit("dve", lambda e, bi=bi, t16=t16, ex=ex:
                                              e.scalar_tensor_tensor(
                                    acc[:, t16 * 1024:t16 * 1024 + 1024],
                                    PD[bi][:],
                                    gate[:, t16 * 8 + ex:t16 * 8 + ex + 1],
                                    acc[:, t16 * 1024:t16 * 1024 + 1024],
                                    ALU.mult, ALU.add),
                                    after=[y_last[ts], acc_ev[t16],
                                           e_gate_strip[g]], ms=True)
                                ybank[bi - 1] = [e_ya]
                                acc_ev[t16] = e_ya
                    w1_ring.release(pslot, lastmm_piece)
                    w2_ring.release(w2par, last_y_mm[0])
                _stop(f"p{piece}")

            e_expert_done = [last_y_mm[0]] + [acc_ev[t] for t in range(16)]
            e_spill_tot = [S.dma_event("s0"), S.dma_event("s1")]
            out_evs = []
            for t16 in range(16):
                slot, rel = x_ring.acquire()
                e_x = S.emit("sp", lambda e, slot=slot, t16=t16: e.dma_start(
                    out=x_st[slot][:], in_=xr[t16 * 128:(t16 + 1) * 128, :]),
                    after=[rel], dma=f"x{slot}")
                sslot, srel = s_ring.acquire()
                e_s = S.emit("sp", lambda e, sslot=sslot, t16=t16: e.dma_start(
                    out=s_stg[sslot][:], in_=s_scr[t16 * 128:(t16 + 1) * 128, :]),
                    after=[srel] + e_spill_tot + e_expert_done, dma=f"s{sslot}")
                gslot, grel = prod_ring.acquire()
                e_g = S.emit("act", lambda e, gslot=gslot, t16=t16: e.activation(
                    prod[gslot][:], acc[:, t16 * 1024:t16 * 1024 + 1024], AF.Silu),
                    after=[acc_ev[t16], grel] + e_expert_done, ms=True)
                e_sg = S.emit("dve", lambda e, gslot=gslot, sslot=sslot: e.scalar_tensor_tensor(
                    prod[gslot][:], s_stg[sslot][:], consts[:, 6:7], prod[gslot][:],
                    ALU.mult, ALU.mult), after=[e_g, e_s, e_c], ms=True)
                e_o = S.emit("dve", lambda e, slot=slot, gslot=gslot: e.tensor_add(
                    x_st[slot][:], x_st[slot][:], prod[gslot][:]), after=[e_sg, e_x], ms=True)
                prod_ring.release(gslot, e_o)
                e_od = S.emit("sp", lambda e, t16=t16, slot=slot: e.dma_start(
                    out=out_d[t16 * 128:(t16 + 1) * 128, :], in_=x_st[slot][:]),
                    after=[e_o], dma=f"o{slot}")
                x_ring.release(slot, e_od)
                s_ring.release(sslot, e_od)
                out_evs.append(e_od)

            S.emit("sp", lambda e: e.nop(),
                   after=[S.dma_event("o0"), S.dma_event("o1"), S.dma_event("s0"),
                          S.dma_event("s1"), S.dma_event("x0"), S.dma_event("x1")])


        except _TruncEx:
            pass

        with nc.Block() as block:
            @block.tensor
            def _(e): S.run("pe", e)

            @block.scalar
            def _(e): S.run("act", e)

            @block.vector
            def _(e): S.run("dve", e)

            @block.gpsimd
            def _(e): S.run("pool", e)

            @block.sync
            def _(e): S.run("sp", e)

    return nc


_prog_cache = {}
TRUNC = None
XNOPS = 0


def kernel(L0_attn_x, L0_attn_phase, L0_attn_Wq, L0_attn_bq, L0_attn_Wk, L0_attn_bk,
           L0_attn_Wv, L0_attn_bv, L0_attn_Wo, L0_attn_bo, L0_ffn_W1, L0_ffn_W2, L0_ffn_Wr):
    x = np.ascontiguousarray(np.asarray(L0_attn_x, dtype=np.float32))
    assert x.shape == (2, 8192, 1024)

    def panels(W):
        return np.ascontiguousarray(
            np.asarray(W, np.float32).reshape(8, 128, 8, 512).transpose(2, 1, 0, 3))

    wq_p = panels(L0_attn_Wq)
    wk_p = panels(L0_attn_Wk)
    wv_p = panels(L0_attn_Wv)
    wo_h = np.ascontiguousarray(np.asarray(L0_attn_Wo, np.float32))
    w1t = np.ascontiguousarray(np.asarray(L0_ffn_W1, np.float32).transpose(0, 2, 1))
    w2t = np.ascontiguousarray(np.asarray(L0_ffn_W2, np.float32).transpose(0, 2, 1))
    wrt_h = np.ascontiguousarray(
        np.asarray(L0_ffn_Wr, np.float32).T.reshape(8, 128, 8).transpose(1, 0, 2).reshape(128, 64))
    cols = lambda v: np.ascontiguousarray(np.asarray(v, np.float32).reshape(32, 128).T)
    phct_h = cols(np.asarray(L0_attn_phase, np.float32).reshape(4096))
    bqc_h = cols(L0_attn_bq)
    bkc_h = cols(L0_attn_bk)
    bvc_h = cols(L0_attn_bv)
    bo_h = np.ascontiguousarray(np.asarray(L0_attn_bo, np.float32).reshape(1, 1024))
    sel_h = np.zeros((128, 256), dtype=ml_dtypes.bfloat16)
    for p in range(16):
        sel_h[:, p * 16 + p] = 1
    eyeb_h = np.eye(128, dtype=ml_dtypes.bfloat16)
    eyef_h = np.eye(128, dtype=np.float32)
    ones_h = np.ones((1, 128), dtype=ml_dtypes.bfloat16)

    if "nc" not in _prog_cache:
        _prog_cache["nc"] = build_program()
    nc = _prog_cache["nc"]

    in_maps = []
    for core in range(8):
        b, c2 = core // 4, core % 4
        xf = np.ascontiguousarray(x[b, 2048 * c2: 2048 * (c2 + 1), :])
        strips = [x[b, h * 2048 + 512 * c2: h * 2048 + 512 * (c2 + 1), :] for h in range(4)]
        xrr = np.ascontiguousarray(np.concatenate(strips, axis=0))
        in_maps.append({
            "xf": xf, "xr": xrr, "wq_p": wq_p, "wk_p": wk_p, "wv_p": wv_p,
            "wo_h": wo_h, "w1t": w1t, "w2t": w2t, "wrt_h": wrt_h, "phct_h": phct_h,
            "bqc_h": bqc_h, "bkc_h": bkc_h, "bvc_h": bvc_h, "bo_h": bo_h,
            "sel_h": sel_h, "eyeb_h": eyeb_h, "eyef_h": eyef_h, "ones_h": ones_h,
        })

    res = run_bass_kernel_spmd(nc, in_maps, core_ids=list(range(8)))
    out = np.empty((2, 8192, 1024), dtype=np.float32)
    for core in range(8):
        b, c2 = core // 4, core % 4
        r = np.asarray(res.results[core]["out"]).reshape(4, 512, 1024)
        for h in range(4):
            out[b, h * 2048 + 512 * c2: h * 2048 + 512 * (c2 + 1), :] = r[h]
    return out


# version 13
# speedup vs baseline: 1.0426x; 1.0195x over previous
import contextlib
import numpy as np
import ml_dtypes
import concourse.bass as bass
import concourse.mybir as mybir
from concourse.bass_utils import run_bass_kernel_spmd

F32, BF16 = mybir.dt.float32, mybir.dt.bfloat16
AF = mybir.ActivationFunctionType
ALU = mybir.AluOpType
AX = mybir.AxisListType

D = 1024
H = 4
NG = 4
TPG = 512
NT = 4
E = 8
FQ = 1024
NQ = 4
PI2 = 1.5707963267948966


class Sched:
    ENG = ("pe", "act", "dve", "pool", "sp")

    def __init__(self, nc, ctx):
        self.nc = nc
        self.ctx = ctx
        self.ops = {e: [] for e in self.ENG}
        self.sem = {e: ctx.enter_context(nc.semaphore(name='sem_' + e)) for e in self.ENG}
        self.cnt = {e: 0 for e in self.ENG}
        self.maxwait = {e: {} for e in self.ENG}

    def _sem(self, name):
        if name not in self.sem:
            self.sem[name] = self.ctx.enter_context(
                self.nc.semaphore(name='sem_' + name))
            self.cnt[name] = 0
        return self.sem[name]

    def dma_event(self, stream):
        self._sem(stream)
        return (stream, self.cnt[stream])

    def emit(self, eng, fn, after=(), ms=False, dma=None):
        waits = []
        flat = []
        for ev in after:
            if ev is None:
                continue
            if isinstance(ev, list):
                flat += [e for e in ev if e is not None]
            else:
                flat.append(ev)
        for (pname, val) in flat:
            self._sem(pname)
            if val > self.maxwait[eng].get(pname, 0):
                waits.append((self.sem[pname], val))
                self.maxwait[eng][pname] = val
        inc = None
        if dma is not None:
            self._sem(dma)
            self.cnt[dma] += 16
            inc = (self.sem[dma], 16)
            self.ops[eng].append((fn, waits, inc))
            return (dma, self.cnt[dma])
        if ms:
            self.cnt[eng] += 1
            inc = (self.sem[eng], 1)
        self.ops[eng].append((fn, waits, inc))
        return (eng, self.cnt[eng])

    def run(self, eng, engine):
        for fn, waits, inc in self.ops[eng]:
            for (sem, val) in waits:
                engine.wait_ge(sem, val)
            inst = fn(engine)
            if inc is not None:
                inst.then_inc(inc[0], inc[1])


class Ring:
    def __init__(self, n):
        self.n = n
        self.last = [None] * n
        self.i = 0

    def acquire(self):
        s = self.i % self.n
        self.i += 1
        return s, self.last[s]

    def release(self, s, ev):
        self.last[s] = ev


class _TruncEx(Exception):
    pass


def build_program():
    def _stop(tag):
        if TRUNC is not None and TRUNC == tag:
            raise _TruncEx()
    nc = bass.Bass("TRN2", target_bir_lowering=False)

    xf = nc.dram_tensor("xf", [2048, D], F32, kind="ExternalInput").ap()
    xr = nc.dram_tensor("xr", [2048, D], F32, kind="ExternalInput").ap()
    wq_p = nc.dram_tensor("wq_p", [8, 128, 8, 512], F32, kind="ExternalInput").ap()
    wk_p = nc.dram_tensor("wk_p", [8, 128, 8, 512], F32, kind="ExternalInput").ap()
    wv_p = nc.dram_tensor("wv_p", [8, 128, 8, 512], F32, kind="ExternalInput").ap()
    wo_h = nc.dram_tensor("wo_h", [4096, D], F32, kind="ExternalInput").ap()
    w1t = nc.dram_tensor("w1t", [E, D, 4096], F32, kind="ExternalInput").ap()
    w2t = nc.dram_tensor("w2t", [E, 4096, D], F32, kind="ExternalInput").ap()
    wrt_h = nc.dram_tensor("wrt_h", [128, 64], F32, kind="ExternalInput").ap()
    phct_h = nc.dram_tensor("phct_h", [128, 32], F32, kind="ExternalInput").ap()
    bqc_h = nc.dram_tensor("bqc_h", [128, 32], F32, kind="ExternalInput").ap()
    bkc_h = nc.dram_tensor("bkc_h", [128, 32], F32, kind="ExternalInput").ap()
    bvc_h = nc.dram_tensor("bvc_h", [128, 32], F32, kind="ExternalInput").ap()
    bo_h = nc.dram_tensor("bo_h", [1, D], F32, kind="ExternalInput").ap()
    sel_h = nc.dram_tensor("sel_h", [128, 256], BF16, kind="ExternalInput").ap()
    eyeb_h = nc.dram_tensor("eyeb_h", [128, 128], BF16, kind="ExternalInput").ap()
    eyef_h = nc.dram_tensor("eyef_h", [128, 128], F32, kind="ExternalInput").ap()
    ones_h = nc.dram_tensor("ones_h", [1, 128], BF16, kind="ExternalInput").ap()
    s_scr = nc.dram_tensor("s_scr", [2048, D], BF16, kind="Internal").ap()
    out_d = nc.dram_tensor("out", [2048, D], F32, kind="ExternalOutput").ap()

    ctx = contextlib.ExitStack()
    with ctx:
        sb = lambda name, shape, dt: ctx.enter_context(nc.sbuf_tensor(name, shape, dt))
        psa = lambda name, shape, dt: ctx.enter_context(nc.psum_tensor(name, shape, dt))
        S = Sched(nc, ctx)

        AB = sb("AB", [128, 40960], BF16)
        AFr = sb("AFr", [128, 16384], F32)

        x_st = [sb(f"x_st{i}", [128, D], F32) for i in range(2)]
        xn_bf = []
        xnT_f = sb("xnT_f", [128, 4096], BF16)
        wpan = [sb(f"wpan{i}", [128, 4096], BF16) for i in range(2)]
        wor = [sb(f"wor{i}", [128, 2048], BF16) for i in range(2)]
        prod = [sb(f"prod{i}", [128, 1024], BF16) for i in range(2)]
        s_stg = [sb(f"s_stg{i}", [128, D], BF16) for i in range(2)]
        vt_st = [sb(f"vt_st{i}", [128, 512], BF16) for i in range(2)]
        stats = sb("stats", [128, 64], F32)
        consts = sb("consts", [128, 8], F32)
        phcc = sb("phcc", [128, 32], F32)
        psct = sb("psct", [128, 32], F32)
        bqc = sb("bqc", [128, 32], F32)
        bkc = sb("bkc", [128, 32], F32)
        bvc = sb("bvc", [128, 32], F32)
        wcl = sb("wcl", [128, 128], F32)
        wtmp = sb("wtmp", [128, 8], F32)
        sco = sb("sco", [16, 512], F32)
        lgt = sco[0:8, :]
        atn = sb("atn", [128, 64], F32)
        smx = sb("smx", [128, 16], F32)
        zs = sb("zs", [128, 16], F32)
        zr = sb("zr", [128, 16], F32)
        lgtm = sb("lgtm", [128, 32], F32)
        gate = sb("gate", [128, 128], F32)
        wrt_s = sb("wrt_s", [128, 64], BF16)
        bo_bf = sb("bo_bf", [1, D], BF16)
        sel_s = sb("sel_s", [128, 256], BF16)
        eyeb = sb("eyeb", [128, 128], BF16)
        eyef = sb("eyef", [128, 128], F32)
        ones_r = sb("ones_r", [1, 128], BF16)
        gct = sb("gct", [128, 16], F32)

        PD = [psa(f"PD{i}", [128, 1024], F32) for i in range(3)]
        PB3 = psa("PB3", [128, 2048], BF16)
        bank_ev = {}

        def bank_guard(key):
            return bank_ev.get(key, [])

        ld = []
        ld.append(S.emit("pool", lambda e: e.dma_start(out=wrt_s[:], in_=wrt_h[:, :]), dma="miscp"))
        ld.append(S.emit("sp", lambda e: e.dma_start(out=phcc[:], in_=phct_h[:, :]), dma="misc"))
        ld.append(S.emit("sp", lambda e: e.dma_start(out=bqc[:], in_=bqc_h[:, :]), dma="misc"))
        ld.append(S.emit("sp", lambda e: e.dma_start(out=bkc[:], in_=bkc_h[:, :]), dma="misc"))
        ld.append(S.emit("sp", lambda e: e.dma_start(out=bvc[:], in_=bvc_h[:, :]), dma="misc"))
        ld.append(S.emit("sp", lambda e: e.dma_start(out=sel_s[:], in_=sel_h[:, :]), dma="misc"))
        ld.append(S.emit("sp", lambda e: e.dma_start(out=eyeb[:], in_=eyeb_h[:, :]), dma="misc"))
        ld.append(S.emit("sp", lambda e: e.dma_start(out=eyef[:], in_=eyef_h[:, :]), dma="misc"))
        ld.append(S.emit("sp", lambda e: e.dma_start(out=ones_r[:], in_=ones_h[:, :]), dma="misc"))
        e_bo = S.emit("pool", lambda e: e.dma_start(out=bo_bf[:], in_=bo_h[:, :]), dma="miscp")
        e_misc = [S.dma_event("misc"), S.dma_event("miscp")]

        cvals = [PI2, -1.0 / D, 1.0 / D, 1e-5, -0.5, 1.0 / 32.0, 1.0, 0.0]
        e_c = None
        for i, v in enumerate(cvals):
            e_c = S.emit("dve", lambda e, i=i, v=v: e.memset(consts[:, i:i + 1], float(v)),
                         after=[e_c] if e_c else [], ms=True)

        e_sin = S.emit("act", lambda e: e.activation(psct[:], phcc[:], AF.Sin),
                       after=[e_misc, e_c], ms=True)
        e_cos = S.emit("act", lambda e: e.activation(phcc[:], phcc[:], AF.Sin, bias=consts[:, 0:1]),
                       after=[e_sin], ms=True)
        e_w = e_cos
        for pr in range(16):
            hh, gg2 = pr // 4, pr % 4
            e_w1_ = S.emit("dve", lambda e, hh=hh, gg2=gg2, pr=pr: e.tensor_mul(
                wcl[:, pr * 8:pr * 8 + 8], phcc[:, hh * 8:hh * 8 + 8],
                phcc[:, gg2 * 8:gg2 * 8 + 8]), after=[e_w], ms=True)
            e_w2_ = S.emit("dve", lambda e, hh=hh, gg2=gg2: e.tensor_mul(
                wtmp[:], psct[:, hh * 8:hh * 8 + 8], psct[:, gg2 * 8:gg2 * 8 + 8]),
                after=[e_w1_], ms=True)
            e_w = S.emit("dve", lambda e, pr=pr: e.tensor_add(
                wcl[:, pr * 8:pr * 8 + 8], wcl[:, pr * 8:pr * 8 + 8], wtmp[:]),
                after=[e_w2_], ms=True)
        e_trig = [e_w]

        ln_chain = [None]
        ln_slot_ev = {}
        ln_slot_ctr = [0]
        ln_sq_ev = [None]

        def emit_ln(x_ev, x_slot, xn_slot, extra_after=()):
            sl = ln_slot_ctr[0] % 8
            ln_slot_ctr[0] += 1
            c0 = sl * 8
            after0 = [x_ev, e_c, ln_slot_ev.get(sl), ln_sq_ev[0]] + list(extra_after)
            e1 = S.emit("act", lambda e: e.activation(sqd, x_st[x_slot][:], AF.Identity,
                                                      accum_out=stats[:, c0:c0 + 1]),
                        after=after0, ms=True)
            e2 = S.emit("act", lambda e: e.activation(stats[:, c0 + 1:c0 + 2],
                                                      stats[:, c0:c0 + 1], AF.Identity,
                                                      scale=consts[:, 1:2]), after=[e1], ms=True)
            e3 = S.emit("act", lambda e: e.activation(x_st[x_slot][:], x_st[x_slot][:],
                                                      AF.Identity,
                                                      bias=stats[:, c0 + 1:c0 + 2]),
                        after=[e2], ms=True)
            e4 = S.emit("act", lambda e: e.activation(sqd, x_st[x_slot][:], AF.Square,
                                                      accum_out=stats[:, c0 + 2:c0 + 3]),
                        after=[e3], ms=True)
            ln_sq_ev[0] = e4
            e5 = S.emit("act", lambda e: e.activation(stats[:, c0 + 3:c0 + 4],
                                                      stats[:, c0 + 2:c0 + 3], AF.Ln,
                                                      scale=consts[:, 2:3], bias=consts[:, 3:4]),
                        after=[e4], ms=True)
            e6 = S.emit("act", lambda e: e.activation(stats[:, c0 + 4:c0 + 5],
                                                      stats[:, c0 + 3:c0 + 4], AF.Exp,
                                                      scale=consts[:, 4:5]), after=[e5], ms=True)
            e7 = S.emit("act", lambda e: e.activation(xn_bf[xn_slot][:], x_st[x_slot][:],
                                                      AF.Identity,
                                                      scale=stats[:, c0 + 4:c0 + 5]),
                        after=[e6], ms=True)
            ln_slot_ev[sl] = e7
            ln_chain[0] = e7
            return e7

        def emit_xn_transpose(xn_ev, xn_slot, dstT_ap, cstride, dst_col, extra_after=()):
            evs = []
            for cc in range(2):
                half = cc
                key = ("PD2", half)
                aft = [xn_ev] + list(extra_after) + bank_guard(key)
                last = None
                for c in range(4 * cc, 4 * cc + 4):
                    off = half * 512 + (c % 4) * 128
                    last = S.emit("pe", lambda e, c=c, off=off: e.transpose(
                        PD[2][:, off:off + 128], xn_bf[xn_slot][:, c * 128:(c + 1) * 128],
                        eyef[:]), after=aft, ms=(c == 4 * cc + 3))
                    aft = []
                ev = S.emit("act", lambda e, cc=cc, half=half: e.copy(
                    dstT_ap.rearrange("p (c t) -> p c t", c=8 * (4096 // cstride) // 8 if False
                                      else (4096 * 8 // (cstride * 8)) if False else 8)
                    [:, 4 * cc:4 * cc + 4, dst_col:dst_col + 128]
                    if cstride * 8 == dstT_ap.shape[-1] else None,
                    PD[2][:, half * 512:half * 512 + 512].rearrange("p (c t) -> p c t", c=4)),
                    after=[last], ms=True)
                bank_ev[key] = [ev]
                evs.append(ev)
            return evs[-1]

        def emit_xn_transpose2(xn_ev, xn_slot, dstT_ap, cstride, dst_col, extra_after=()):
            evs = []
            dst3 = dstT_ap.rearrange("p (c t) -> p c t", t=cstride)
            for cc in range(2):
                half = cc
                key = ("PD2", half)
                aft = [xn_ev] + list(extra_after) + bank_guard(key)
                last = None
                for c in range(4 * cc, 4 * cc + 4):
                    off = half * 512 + (c % 4) * 128
                    last = S.emit("pe", lambda e, c=c, off=off: e.transpose(
                        PD[2][:, off:off + 128], xn_bf[xn_slot][:, c * 128:(c + 1) * 128],
                        eyef[:]), after=aft, ms=(c == 4 * cc + 3))
                    aft = []
                ev = S.emit("act", lambda e, cc=cc, half=half: e.copy(
                    dst3[:, 4 * cc:4 * cc + 4, dst_col:dst_col + 128],
                    PD[2][:, half * 512:half * 512 + 512].rearrange("p (c t) -> p c t", c=4)),
                    after=[last], ms=True)
                bank_ev[key] = [ev]
                evs.append(ev)
            return evs[-1]

        def proj_chunk(nc_idx, pan_slot, pan_ev, xnT_ap, xnT_ev, pp):
            key = ("PD0", pp)
            aft = [pan_ev, xnT_ev] + bank_guard(key)
            last = None
            for dc in range(8):
                lhs = wpan[pan_slot][:, dc * 512 + (nc_idx % 4) * 128:
                                     dc * 512 + (nc_idx % 4) * 128 + 128]
                last = S.emit("pe", lambda e, lhs=lhs, dc=dc, pp=pp: e.matmul(
                    PD[0][:, pp * 512:pp * 512 + 512], lhs,
                    xnT_ap[:, dc * 512:dc * 512 + 512],
                    start=(dc == 0), stop=(dc == 7)),
                    after=aft if dc == 0 else [], ms=(dc == 7))
                aft = []
            return last

        try:
            kbT = AB[:, 0:16384]
            qbT = AB[:, 32768:36864]
            qw_sl = [AB[:, 36864:37888], AB[:, 37888:38912]]
            v_tm = AB[:, 0:16384]
            outT = AB[:, 16384:32768]
            out_tm = [AFr[:, 0:4096], AFr[:, 4096:8192]]
            out_bf = [AB[:, 32768:36864], AB[:, 36864:40960]]
            sqd = AFr[:, 8192:9216]
            xn_bf.append(AFr[:, 9216:10240])
            xn_bf.append(AFr[:, 10240:11264])
            xn_bf.append(AFr[:, 11264:12288])
            xn_bf.append(AFr[:, 12288:13312])
            x_st.append(AFr[:, 13312:14336])
            x_st.append(AFr[:, 14336:15360])

            pan_ring = Ring(2)
            prod_sl = [prod[0][:, 0:512], prod[0][:, 512:1024],
                       prod[1][:, 0:512], prod[1][:, 512:1024]]
            prod_ev = {i: None for i in range(4)}
            wo_ring = Ring(2)
            prod_ring = Ring(2)
            x_ring = Ring(4)
            xn_ring = Ring(4)
            vt_ring = Ring(2)
            s_ring = Ring(2)

            e_spill_all = []
            group_wo_done = None
            e_xnT_ffn_evs = []

            sc_state = {"first": True, "aft0": None, "last": None}

            def emit_front_ln_tile(g, tt):
                slot, rel = x_ring.acquire()
                e_x = S.emit("sp", lambda e, slot=slot, g=g, tt=tt: e.dma_start(
                    out=x_st[slot][:], in_=xf[g * TPG + tt * 128: g * TPG + (tt + 1) * 128, :]),
                    after=[rel], dma=f"x{slot}")
                xnslot, xnrel = xn_ring.acquire()
                e_ln = emit_ln(e_x, slot, xnslot, extra_after=[xnrel] + e_trig)
                x_ring.release(slot, e_ln)
                return (e_ln, xnslot)

            def emit_front_xnT(lns):
                last = None
                for tt, (e_ln, xnslot) in enumerate(lns):
                    e_tr = emit_xn_transpose2(e_ln, xnslot, xnT_f[:], 512, tt * 128)
                    xn_ring.release(xnslot, e_tr)
                    last = e_tr
                return last

            pend = [emit_front_ln_tile(0, tt) for tt in range(NT)]
            for g in range(NG):
                e_xnT = emit_front_xnT(pend)
                pend_next = []
                _stop(f"g{g}_ln")

                k_ev_last = []
                for pan in range(8):
                    pslot, prel = pan_ring.acquire()
                    e_pan = S.emit("pool", lambda e, pslot=pslot, pan=pan: e.dma_start(
                        out=wpan[pslot][:].rearrange("p (c n) -> p c n", c=8), in_=wk_p[pan]),
                        after=[prel], dma=f"pan{pslot}")
                    lastmm = None
                    for sub in range(4):
                        nc_idx = pan * 4 + sub
                        pp = nc_idx % 2
                        e_mm = proj_chunk(nc_idx, pslot, e_pan, xnT_f[:], e_xnT, pp)
                        if nc_idx % 2 == 0:
                            e_v1 = S.emit("dve", lambda e, nc_idx=nc_idx, pp=pp:
                                          e.tensor_scalar_add(
                                kbT[:, nc_idx * 512:nc_idx * 512 + 512],
                                PD[0][:, pp * 512:pp * 512 + 512],
                                bkc[:, nc_idx:nc_idx + 1]), after=[e_mm], ms=True)
                        else:
                            e_v1 = S.emit("act", lambda e, nc_idx=nc_idx, pp=pp: e.activation(
                                kbT[:, nc_idx * 512:nc_idx * 512 + 512],
                                PD[0][:, pp * 512:pp * 512 + 512], AF.Identity,
                                bias=bkc[:, nc_idx:nc_idx + 1]), after=[e_mm], ms=True)
                        bank_ev[("PD0", pp)] = [e_v1]
                        lastmm = e_mm
                        k_ev_last = [e_v1]
                    pan_ring.release(pslot, lastmm)
                _stop(f"g{g}_k")

                for h in range(H):
                    q_ev_last = []
                    lastmm = {}
                    pans = {}
                    for hp in range(2):
                        pslot, prel = pan_ring.acquire()
                        e_pan = S.emit("pool", lambda e, pslot=pslot, h=h, hp=hp: e.dma_start(
                            out=wpan[pslot][:].rearrange("p (c n) -> p c n", c=8),
                            in_=wq_p[2 * h + hp]), after=[prel], dma=f"pan{pslot}")
                        pans[hp] = (pslot, e_pan)
                    e_pan_all = [S.dma_event("pan0"), S.dma_event("pan1")]
                    q_ev_byc = {}
                    for sub in range(8):
                        nc_idx = h * 8 + sub
                        pslot, e_pan = pans[sub // 4]
                        pp = nc_idx % 2
                        e_mm = proj_chunk(nc_idx, pslot, e_pan_all, xnT_f[:], e_xnT, pp)
                        lastmm[sub // 4] = e_mm
                        e_q2 = S.emit("act", lambda e, nc_idx=nc_idx, sub=sub, pp=pp: e.activation(
                            qbT[:, sub * 512:sub * 512 + 512],
                            PD[0][:, pp * 512:pp * 512 + 512], AF.Identity,
                            bias=bqc[:, nc_idx:nc_idx + 1]), after=[e_mm], ms=True)
                        bank_ev[("PD0", pp)] = [e_q2]
                        q_ev_byc[sub] = e_q2
                        q_ev_last = [e_q2]
                    pan_ring.release(pans[0][0], lastmm[0])
                    pan_ring.release(pans[1][0], lastmm[1])
                    for gg in range(H):
                        pair = h * 4 + gg
                        for ck in range(8):
                            prslot = ck % 4
                            prrel = prod_ev[prslot]
                            e_pr = S.emit("dve", lambda e, prslot=prslot, ck=ck, gg=gg,
                                          pair=pair: e.scalar_tensor_tensor(
                                prod_sl[prslot],
                                qbT[:, ck * 512:ck * 512 + 512],
                                wcl[:, pair * 8 + ck:pair * 8 + ck + 1],
                                kbT[:, (gg * 8 + ck) * 512:(gg * 8 + ck) * 512 + 512],
                                ALU.mult, ALU.mult),
                                after=[q_ev_byc[ck], k_ev_last, prrel], ms=True)
                            aftm = [e_pr, e_misc]
                            if sc_state["first"] and sc_state["aft0"] is not None:
                                aftm.append(sc_state["aft0"])
                            is_last = (pair == 15 and ck == 7)
                            e_smm = S.emit("pe", lambda e, prslot=prslot, pair=pair,
                                           st=sc_state["first"], sp_=is_last: e.matmul(
                                PD[1][0:16, 0:512],
                                sel_s[:, pair * 16:pair * 16 + 16],
                                prod_sl[prslot],
                                start=st, stop=sp_),
                                after=aftm, ms=True)
                            sc_state["first"] = False
                            prod_ev[prslot] = e_smm
                            sc_state["last"] = e_smm
                        if g + 1 < NG and gg == 3 and len(pend_next) < NT:
                            pend_next.append(emit_front_ln_tile(g + 1, len(pend_next)))

                _stop(f"g{g}_q")
                e_sc = S.emit("dve", lambda e: e.tensor_scalar_mul(sco[:], PD[1][0:16, 0:512],
                                                                   consts[0:16, 5:6]),
                              after=[sc_state["last"], e_c], ms=True)
                sc_state["first"] = True
                sc_state["aft0"] = e_sc
                atn_evs = []
                for tt in range(NT):
                    key = ("PD1T",)
                    aft = [e_sc] + bank_guard(key)
                    e_t = S.emit("pe", lambda e, tt=tt: e.transpose(
                        PD[1][:, 512 + tt * 16:512 + tt * 16 + 16],
                        sco[:, tt * 128:tt * 128 + 128], eyef[0:16, 0:16]), after=aft, ms=True)
                    e_a = S.emit("dve", lambda e, tt=tt: e.tensor_copy(
                        atn[:, tt * 16:tt * 16 + 16], PD[1][:, 512 + tt * 16:512 + tt * 16 + 16]),
                        after=[e_t], ms=True)
                    bank_ev[key] = [e_a]
                    atn_evs.append(e_a)
                smax_evs = []
                for tt in range(NT):
                    for h in range(H):
                        c0 = tt * 16 + h * 4
                        e_m = S.emit("dve", lambda e, c0=c0, tt=tt, h=h: e.reduce_max(
                            smx[:, tt * 4 + h:tt * 4 + h + 1], atn[:, c0:c0 + 4], AX.X, negate=True),
                            after=[atn_evs[tt]], ms=True)
                        e_e = S.emit("act", lambda e, c0=c0, tt=tt, h=h: e.activation(
                            atn[:, c0:c0 + 4], atn[:, c0:c0 + 4], AF.Exp,
                            bias=smx[:, tt * 4 + h:tt * 4 + h + 1],
                            accum_out=zs[:, tt * 4 + h:tt * 4 + h + 1]), after=[e_m], ms=True)
                        smax_evs.append(e_e)
                e_zr = S.emit("dve", lambda e: e.reciprocal(zr[:], zs[:]), after=smax_evs, ms=True)
                e_attn = None
                for tt in range(NT):
                    for h in range(H):
                        c0 = tt * 16 + h * 4
                        e_attn = S.emit("dve", lambda e, c0=c0, tt=tt, h=h: e.tensor_scalar_mul(
                            atn[:, c0:c0 + 4], atn[:, c0:c0 + 4],
                            zr[:, tt * 4 + h:tt * 4 + h + 1]), after=[e_zr], ms=True)

                _stop(f"g{g}_smax")
                v_last = None
                for pan in range(8):
                    pslot, prel = pan_ring.acquire()
                    e_pan = S.emit("pool", lambda e, pslot=pslot, pan=pan: e.dma_start(
                        out=wpan[pslot][:].rearrange("p (c n) -> p c n", c=8), in_=wv_p[pan]),
                        after=[prel], dma=f"pan{pslot}")
                    lastmm = None
                    for sub in range(4):
                        nc_idx = pan * 4 + sub
                        pp = nc_idx % 2
                        e_mm = proj_chunk(nc_idx, pslot, e_pan, xnT_f[:], e_xnT, pp)
                        lastmm = e_mm
                        vslot, vrel = vt_ring.acquire()
                        e_ve = S.emit("act", lambda e, nc_idx=nc_idx, pp=pp, vslot=vslot: e.activation(
                            vt_st[vslot][:], PD[0][:, pp * 512:pp * 512 + 512], AF.Identity,
                            bias=bvc[:, nc_idx:nc_idx + 1]), after=[e_mm, vrel], ms=True)
                        bank_ev[("PD0", pp)] = [e_ve]
                        half = nc_idx % 2
                        keyt = ("PB3", half)
                        aftt = [e_ve] + bank_guard(keyt)
                        last = None
                        for b4 in range(4):
                            last = S.emit("pe", lambda e, b4=b4, half=half, vslot=vslot: e.transpose(
                                PB3[:, half * 1024 + b4 * 128:half * 1024 + b4 * 128 + 128],
                                vt_st[vslot][:, b4 * 128:b4 * 128 + 128], eyeb[:]),
                                after=aftt, ms=(b4 == 3))
                            aftt = []
                        vt_ring.release(vslot, last)
                        e_vt = S.emit("dve", lambda e, nc_idx=nc_idx, half=half: e.tensor_copy(
                            v_tm.rearrange("p (t n) -> p t n", t=4)[:, :,
                                                                    nc_idx * 128:nc_idx * 128 + 128],
                            PB3[:, half * 1024:half * 1024 + 512].rearrange("p (t n) -> p t n", t=4)),
                            after=[last], ms=True)
                        bank_ev[keyt] = [e_vt]
                        v_last = e_vt
                    pan_ring.release(pslot, lastmm)

                _stop(f"g{g}_v")
                out_ev_tiles = []
                for tt in range(NT):
                    obf = out_bf[tt % 2]
                    prev_ot = out_ev_tiles[tt - 2] if tt >= 2 else None
                    ao_evs = []
                    for h in range(H):
                        e_ao = S.emit("dve", lambda e, tt=tt, h=h, obf=obf: e.tensor_scalar_mul(
                            obf[:, h * 1024:h * 1024 + 1024],
                            v_tm[:, tt * 4096:tt * 4096 + 1024],
                            atn[:, tt * 16 + h * 4:tt * 16 + h * 4 + 1]),
                            after=[v_last, e_attn, prev_ot], ms=True)
                        for gg in range(1, 4):
                            e_ao = S.emit("dve", lambda e, tt=tt, h=h, gg=gg, obf=obf:
                                          e.scalar_tensor_tensor(
                                obf[:, h * 1024:h * 1024 + 1024],
                                v_tm[:, tt * 4096 + gg * 1024:tt * 4096 + gg * 1024 + 1024],
                                atn[:, tt * 16 + h * 4 + gg:tt * 16 + h * 4 + gg + 1],
                                obf[:, h * 1024:h * 1024 + 1024],
                                ALU.mult, ALU.add), after=[e_ao], ms=True)
                        ao_evs.append(e_ao)
                    last_oe = None
                    for oc in range(8):
                        c0 = oc * 4
                        half = oc % 2
                        keyt = ("PB3", half)
                        aftt = [ao_evs[c0 // 8]] + bank_guard(keyt)
                        last = None
                        for b4 in range(4):
                            c = c0 + b4
                            last = S.emit("pe", lambda e, c=c, b4=b4, half=half, obf=obf:
                                          e.transpose(
                                PB3[:, half * 1024 + b4 * 128:half * 1024 + b4 * 128 + 128],
                                obf[:, c * 128:c * 128 + 128], eyeb[:]),
                                after=aftt, ms=(b4 == 3))
                            aftt = []
                        last_oe = S.emit("act", lambda e, c0=c0, half=half, tt=tt: e.copy(
                            outT.rearrange("p (c t) -> p c t", c=32)[:, c0:c0 + 4,
                                                                     tt * 128:tt * 128 + 128],
                            PB3[:, half * 1024:half * 1024 + 512]
                            .rearrange("p (c t) -> p c t", c=4)), after=[last], ms=True)
                        bank_ev[keyt] = [last_oe]
                    out_ev_tiles.append(last_oe)
                e_outT = out_ev_tiles[-1]

                _stop(f"g{g}_ao")
                aft_wo0 = [e_outT]
                for key in [("PD0", 0), ("PD0", 1), ("PD1T",), ("PD1S",), ("PD2", 0),
                            ("PD2", 1), ("PB3", 0), ("PB3", 1)]:
                    aft_wo0 += bank_guard(key)
                aft_wo0.append(sc_state["aft0"])
                wo_last = {}
                first_wo = True
                for nhf in range(2):
                    for kc4 in range(8):
                        wslot, wrel = wo_ring.acquire()
                        e_wo = S.emit("pool", lambda e, wslot=wslot, kc4=kc4, nhf=nhf:
                                      e.dma_start(
                            out=wor[wslot][:].rearrange("p (c n) -> p c n", c=4),
                            in_=wo_h[kc4 * 512:(kc4 + 1) * 512,
                                     nhf * 512:(nhf + 1) * 512]
                            .rearrange("(c p) n -> p c n", p=128)), after=[wrel],
                            dma=f"wo{wslot}")
                        lastuse = None
                        for sub in range(4):
                            kc = kc4 * 4 + sub
                            j, dr = kc // 8, kc % 8
                            for h in range(H):
                                lhs = outT[:, (h * 8 + dr) * 512 + j:
                                           (h * 8 + dr) * 512 + 512: 4]
                                aftm = [e_wo]
                                if first_wo:
                                    aftm += aft_wo0
                                    first_wo = False
                                if kc == 0 and nhf == 1:
                                    aftm.append(wo_last.get(h))
                                e_wm = S.emit("pe", lambda e, h=h, wslot=wslot, kc=kc,
                                              sub=sub, lhs=lhs: e.matmul(
                                    PD[h // 2][:, (h % 2) * 512:
                                               (h % 2) * 512 + 512], lhs,
                                    wor[wslot][:, sub * 512:sub * 512 + 512],
                                    start=(kc == 0), stop=False), after=aftm,
                                    ms=(h == 3))
                                lastuse = e_wm
                                wo_last[h] = e_wm
                        wo_ring.release(wslot, lastuse)
                    for h in range(H):
                        e_bo_mm = S.emit("pe", lambda e, h=h, nhf=nhf: e.matmul(
                            PD[h // 2][:, (h % 2) * 512:(h % 2) * 512 + 512], ones_r[:],
                            bo_bf[:, nhf * 512:nhf * 512 + 512], start=False, stop=True),
                            after=[e_misc, wo_last[h]], ms=True)
                        wo_last[h] = e_bo_mm
                        sslot, srel = s_ring.acquire()
                        e_se = S.emit("dve", lambda e, h=h, sslot=sslot: e.tensor_copy(
                            s_stg[sslot][:, 0:512],
                            PD[h // 2][:, (h % 2) * 512:(h % 2) * 512 + 512]),
                            after=[e_bo_mm, srel], ms=True)
                        e_sp = S.emit("sp", lambda e, h=h, g=g, sslot=sslot, nhf=nhf:
                                      e.dma_start(
                            out=s_scr[h * 512 + g * 128: h * 512 + (g + 1) * 128,
                                      nhf * 512:(nhf + 1) * 512],
                            in_=s_stg[sslot][:, 0:512]), after=[e_se], dma=f"s{sslot}")
                        s_ring.release(sslot, e_sp)
                        e_spill_all.append(e_sp)
                        if nhf == 1:
                            if h == 0:
                                bank_ev[("PD0", 0)] = [e_se]
                            elif h == 1:
                                bank_ev[("PD0", 1)] = [e_se]
                            elif h == 2:
                                bank_ev[("PD1S",)] = [e_se]
                            else:
                                bank_ev[("PD1T",)] = [e_se]
                        wo_last[h] = e_se
                group_wo_done = wo_last[3]
                sc_state["aft0"] = bank_ev[("PD1T",)][0]
                if g + 1 < NG:
                    while len(pend_next) < NT:
                        pend_next.append(emit_front_ln_tile(g + 1, len(pend_next)))
                    pend = pend_next
                bank_ev[("PD2", 0)] = bank_ev.get(("PD2", 0), [])
                bank_ev[("PD2", 1)] = bank_ev.get(("PD2", 1), [])
                _stop(f"g{g}_wo")

            xnT_ffn = AB[:, 0:16384]
            hT = AB[:, 16384:20480]
            w1s = [AB[:, 20480 + i * 8192: 20480 + (i + 1) * 8192] for i in range(2)]
            w2slots = [wpan[0][:, i * 1024:(i + 1) * 1024] for i in range(4)] + \
                      [wpan[1][:, i * 1024:(i + 1) * 1024] for i in range(4)] + \
                      [xnT_f[:, i * 1024:(i + 1) * 1024] for i in range(4)] + \
                      [AB[:, 36864 + i * 2048: 36864 + (i + 1) * 2048][:, 0:1024]
                       for i in range(2)] + [prod[0][:], prod[1][:]]
            acc = AFr[:, 0:16384]

            ffn_guard = [group_wo_done, e_spill_all[-1], e_attn]

            lgt_evs = []
            e_xnT_strip = {}
            e_gate_strip = {}
            for g in range(NG):
                xnT_evs = []
                for tt in range(NT):
                    slot, rel = x_ring.acquire()
                    e_x = S.emit("sp", lambda e, slot=slot, g=g, tt=tt: e.dma_start(
                        out=x_st[slot][:], in_=xr[g * TPG + tt * 128: g * TPG + (tt + 1) * 128, :]),
                        after=[rel], dma=f"x{slot}")
                    xnslot, xnrel = xn_ring.acquire()
                    e_ln = emit_ln(e_x, slot, xnslot, extra_after=[xnrel])
                    x_ring.release(slot, e_ln)
                    e_tr = emit_xn_transpose2(e_ln, xnslot, xnT_ffn, 2048, g * 512 + tt * 128,
                                              extra_after=ffn_guard)
                    xn_ring.release(xnslot, e_tr)
                    xnT_evs.append(e_tr)
                e_xnT_ffn_evs.append(xnT_evs[-1])
                e_xnT_strip[g] = xnT_evs[-1]
                key = ("PD1S",)
                aft = [xnT_evs[-1], e_misc] + bank_guard(key)
                last = None
                for dc in range(8):
                    last = S.emit("pe", lambda e, dc=dc, g=g: e.matmul(
                        PD[1][0:8, 0:512], wrt_s[:, dc * 8:dc * 8 + 8],
                        xnT_ffn[:, dc * 2048 + g * 512:dc * 2048 + g * 512 + 512],
                        start=(dc == 0), stop=(dc == 7)),
                        after=aft if dc == 0 else [], ms=(dc == 7))
                    aft = []
                e_le = S.emit("dve", lambda e: e.tensor_copy(lgt, PD[1][0:8, 0:512]),
                              after=[last], ms=True)
                bank_ev[key] = [e_le]
                for tt in range(NT):
                    keyt = ("PD1T",)
                    aftt = [e_le] + bank_guard(keyt)
                    e_t = S.emit("pe", lambda e, tt=tt: e.transpose(
                        PD[1][:, 512 + tt * 8:512 + tt * 8 + 8],
                        lgt[:, tt * 128:tt * 128 + 128], eyef[0:8, 0:8]), after=aftt, ms=True)
                    e_lt = S.emit("dve", lambda e, tt=tt: e.tensor_copy(
                        lgtm[:, tt * 8:tt * 8 + 8], PD[1][:, 512 + tt * 8:512 + tt * 8 + 8]),
                        after=[e_t], ms=True)
                    bank_ev[keyt] = [e_lt]
                    t16 = g * 4 + tt
                    e_gm = S.emit("dve", lambda e, tt=tt: e.reduce_max(
                        gct[:, 0:1], lgtm[:, tt * 8:tt * 8 + 8], AX.X, negate=True),
                        after=[e_lt], ms=True)
                    e_ge = S.emit("act", lambda e, tt=tt, t16=t16: e.activation(
                        gate[:, t16 * 8:t16 * 8 + 8], lgtm[:, tt * 8:tt * 8 + 8], AF.Exp,
                        bias=gct[:, 0:1]), after=[e_gm], ms=True)
                    e_v1 = S.emit("dve", lambda e, t16=t16: e.reduce_max(
                        gct[:, 1:2], gate[:, t16 * 8:t16 * 8 + 8], AX.X), after=[e_ge], ms=True)
                    e_msk = S.emit("dve", lambda e, t16=t16: e.tensor_single_scalar(
                        gct[:, 4:12], gate[:, t16 * 8:t16 * 8 + 8], gct[:, 1:2], ALU.is_lt),
                        after=[e_v1], ms=True)
                    e_em = S.emit("dve", lambda e, t16=t16: e.tensor_mul(
                        gct[:, 4:12], gct[:, 4:12], gate[:, t16 * 8:t16 * 8 + 8]),
                        after=[e_msk], ms=True)
                    e_v2 = S.emit("dve", lambda e: e.reduce_max(gct[:, 2:3], gct[:, 4:12], AX.X),
                                  after=[e_em], ms=True)
                    e_dn = S.emit("dve", lambda e: e.tensor_add(gct[:, 3:4], gct[:, 1:2],
                                                                gct[:, 2:3]), after=[e_v2], ms=True)
                    e_rd = S.emit("dve", lambda e: e.reciprocal(gct[:, 3:4], gct[:, 3:4]),
                                  after=[e_dn], ms=True)
                    e_gmask = S.emit("dve", lambda e, t16=t16: e.tensor_single_scalar(
                        gct[:, 4:12], gate[:, t16 * 8:t16 * 8 + 8], gct[:, 2:3], ALU.is_ge),
                        after=[e_rd], ms=True)
                    e_gsc = S.emit("dve", lambda e, t16=t16: e.tensor_scalar_mul(
                        gate[:, t16 * 8:t16 * 8 + 8], gate[:, t16 * 8:t16 * 8 + 8], gct[:, 3:4]),
                        after=[e_gmask], ms=True)
                    e_gf = S.emit("dve", lambda e, t16=t16: e.tensor_mul(
                        gate[:, t16 * 8:t16 * 8 + 8], gate[:, t16 * 8:t16 * 8 + 8], gct[:, 4:12]),
                        after=[e_gsc], ms=True)
                    lgt_evs.append(e_gf)
                    e_gate_strip[g] = e_gf
            e_gate = lgt_evs[-1]
            _stop("ffnln")
            e_xnT_all = e_xnT_ffn_evs[-1]

            acc_ev = {}
            for g in range(NG):
                aft_m = ffn_guard + [e_xnT_strip[g]]
                if g >= 2:
                    aft_m += [ln_chain[0], e_xnT_all]
                e_acc0 = S.emit("dve", lambda e, g=g: e.memset(
                    acc[:, g * 4096:(g + 1) * 4096], 0.0), after=aft_m, ms=True)
                for ts in range(4):
                    acc_ev[g * 4 + ts] = e_acc0

            w1_ring = Ring(2)
            w2_ring = Ring(2)
            ybank = {0: bank_guard(("PD1S",)) + bank_guard(("PD1T",)),
                     1: bank_guard(("PD2", 0)) + bank_guard(("PD2", 1))}
            yrot = [0]
            for ex in range(E):
                for qq in range(NQ):
                    piece = ex * NQ + qq
                    pslot, prel = w1_ring.acquire()
                    w2par, w2rel = w2_ring.acquire()
                    w2evs = []
                    for fc in range(8):
                        aft2 = []
                        if fc == 0:
                            aft2 = [w2rel]
                            if piece == 0:
                                aft2 += ffn_guard
                        e_w2c = S.emit("pool", lambda e, w2par=w2par, ex=ex, qq=qq, fc=fc:
                                       e.dma_start(
                            out=w2slots[w2par * 8 + fc],
                            in_=w2t[ex, qq * FQ + fc * 128:qq * FQ + (fc + 1) * 128, :]),
                            after=aft2, dma=f"w2{w2par}")
                        w2evs.append(e_w2c)
                    e_w2 = w2evs[-1]
                    w1evs = []
                    for dc in range(8):
                        aft = []
                        if dc == 0:
                            aft = [prel]
                            if piece == 0:
                                aft += ffn_guard
                        e_w1 = S.emit("pool", lambda e, pslot=pslot, ex=ex, qq=qq, dc=dc: e.dma_start(
                            out=w1s[pslot][:, dc * 1024:(dc + 1) * 1024],
                            in_=w1t[ex, dc * 128:(dc + 1) * 128, qq * FQ:(qq + 1) * FQ]),
                            after=aft, dma=f"w1{pslot}")
                        w1evs.append(e_w1)
                    lastmm_piece = None
                    last_y_mm = [None]
                    for g in range(NG):
                        sil_evs = []
                        for fc in range(8):
                            pp = fc % 2
                            key = ("PD0", pp)
                            aft = [w1evs[-1], e_xnT_strip[g]] + bank_guard(key)
                            last = None
                            for dc in range(8):
                                lhs = w1s[pslot][:, dc * 1024 + fc * 128:dc * 1024 + fc * 128 + 128]
                                last = S.emit("pe", lambda e, lhs=lhs, dc=dc, pp=pp, g=g: e.matmul(
                                    PD[0][:, pp * 512:pp * 512 + 512], lhs,
                                    xnT_ffn[:, dc * 2048 + g * 512:dc * 2048 + g * 512 + 512],
                                    start=(dc == 0), stop=(dc == 7)),
                                    after=aft if dc == 0 else [], ms=(dc == 7))
                                aft = []
                            lastmm_piece = last
                            e_sil = S.emit("act", lambda e, fc=fc, pp=pp: e.activation(
                                hT[:, fc * 512:fc * 512 + 512], PD[0][:, pp * 512:pp * 512 + 512],
                                AF.Silu), after=[last], ms=True)
                            bank_ev[key] = [e_sil]
                            sil_evs.append(e_sil)
                        for pas in range(2):
                            ts_pair = (0, 1) if pas == 0 else (2, 3)
                            tsbank = {}
                            for ts in ts_pair:
                                tsbank[ts] = 1 + (yrot[0] % 2)
                                yrot[0] += 1
                            y_last = {}
                            for ts in ts_pair:
                                bi = tsbank[ts]
                                first = True
                                for fc in range(8):
                                    lhs = hT[:, fc * 512 + ts * 128:fc * 512 + ts * 128 + 128]
                                    for nh in range(2):
                                        aftm = [sil_evs[fc], e_w2]
                                        if first:
                                            aftm += ybank[bi - 1]
                                        e_ym = S.emit("pe", lambda e, bi=bi, nh=nh,
                                                      w2par=w2par, fc=fc, lhs=lhs,
                                                      st=(fc == 0),
                                                      sp_=(fc == 7): e.matmul(
                                            PD[bi][:, nh * 512:nh * 512 + 512], lhs,
                                            w2slots[w2par * 8 + fc][:, nh * 512:nh * 512 + 512],
                                            start=st, stop=sp_), after=aftm, ms=(nh == 1))
                                        first = False
                                    y_last[ts] = e_ym
                                    last_y_mm[0] = e_ym
                            for ts in ts_pair:
                                bi = tsbank[ts]
                                t16 = g * 4 + ts
                                e_ya = S.emit("dve", lambda e, bi=bi, t16=t16, ex=ex:
                                              e.scalar_tensor_tensor(
                                    acc[:, t16 * 1024:t16 * 1024 + 1024],
                                    PD[bi][:],
                                    gate[:, t16 * 8 + ex:t16 * 8 + ex + 1],
                                    acc[:, t16 * 1024:t16 * 1024 + 1024],
                                    ALU.mult, ALU.add),
                                    after=[y_last[ts], acc_ev[t16],
                                           e_gate_strip[g]], ms=True)
                                ybank[bi - 1] = [e_ya]
                                acc_ev[t16] = e_ya
                    w1_ring.release(pslot, lastmm_piece)
                    w2_ring.release(w2par, last_y_mm[0])
                _stop(f"p{piece}")

            e_expert_done = [last_y_mm[0]] + [acc_ev[t] for t in range(16)]
            e_spill_tot = [S.dma_event("s0"), S.dma_event("s1")]
            out_evs = []
            fin_ev = {0: None, 1: None}
            for t16 in range(16):
                slot = t16 % 2
                rel = fin_ev[slot]
                e_x = S.emit("sp", lambda e, slot=slot, t16=t16: e.dma_start(
                    out=x_st[slot][:], in_=xr[t16 * 128:(t16 + 1) * 128, :]),
                    after=[rel, x_ring.last[slot]], dma=f"x{slot}")
                sslot, srel = s_ring.acquire()
                e_s = S.emit("sp", lambda e, sslot=sslot, t16=t16: e.dma_start(
                    out=s_stg[sslot][:], in_=s_scr[t16 * 128:(t16 + 1) * 128, :]),
                    after=[srel] + e_spill_tot + e_expert_done, dma=f"s{sslot}")
                gslot, grel = prod_ring.acquire()
                e_g = S.emit("act", lambda e, gslot=gslot, t16=t16: e.activation(
                    prod[gslot][:], acc[:, t16 * 1024:t16 * 1024 + 1024], AF.Silu),
                    after=[acc_ev[t16], grel] + e_expert_done, ms=True)
                e_sg = S.emit("dve", lambda e, gslot=gslot, sslot=sslot: e.scalar_tensor_tensor(
                    prod[gslot][:], s_stg[sslot][:], consts[:, 6:7], prod[gslot][:],
                    ALU.mult, ALU.mult), after=[e_g, e_s, e_c], ms=True)
                e_o = S.emit("dve", lambda e, slot=slot, gslot=gslot: e.tensor_add(
                    x_st[slot][:], x_st[slot][:], prod[gslot][:]), after=[e_sg, e_x], ms=True)
                prod_ring.release(gslot, e_o)
                e_od = S.emit("sp", lambda e, t16=t16, slot=slot: e.dma_start(
                    out=out_d[t16 * 128:(t16 + 1) * 128, :], in_=x_st[slot][:]),
                    after=[e_o], dma=f"o{slot}")
                x_ring.release(slot, e_od)
                s_ring.release(sslot, e_od)
                out_evs.append(e_od)

            S.emit("sp", lambda e: e.nop(),
                   after=[S.dma_event("o0"), S.dma_event("o1"), S.dma_event("s0"),
                          S.dma_event("s1"), S.dma_event("x0"), S.dma_event("x1")])


        except _TruncEx:
            pass

        with nc.Block() as block:
            @block.tensor
            def _(e): S.run("pe", e)

            @block.scalar
            def _(e): S.run("act", e)

            @block.vector
            def _(e): S.run("dve", e)

            @block.gpsimd
            def _(e): S.run("pool", e)

            @block.sync
            def _(e): S.run("sp", e)

    return nc


_prog_cache = {}
TRUNC = None
XNOPS = 0


def kernel(L0_attn_x, L0_attn_phase, L0_attn_Wq, L0_attn_bq, L0_attn_Wk, L0_attn_bk,
           L0_attn_Wv, L0_attn_bv, L0_attn_Wo, L0_attn_bo, L0_ffn_W1, L0_ffn_W2, L0_ffn_Wr):
    x = np.ascontiguousarray(np.asarray(L0_attn_x, dtype=np.float32))
    assert x.shape == (2, 8192, 1024)

    def panels(W):
        return np.ascontiguousarray(
            np.asarray(W, np.float32).reshape(8, 128, 8, 512).transpose(2, 1, 0, 3))

    wq_p = panels(L0_attn_Wq)
    wk_p = panels(L0_attn_Wk)
    wv_p = panels(L0_attn_Wv)
    wo_h = np.ascontiguousarray(np.asarray(L0_attn_Wo, np.float32))
    w1t = np.ascontiguousarray(np.asarray(L0_ffn_W1, np.float32).transpose(0, 2, 1))
    w2t = np.ascontiguousarray(np.asarray(L0_ffn_W2, np.float32).transpose(0, 2, 1))
    wrt_h = np.ascontiguousarray(
        np.asarray(L0_ffn_Wr, np.float32).T.reshape(8, 128, 8).transpose(1, 0, 2).reshape(128, 64))
    cols = lambda v: np.ascontiguousarray(np.asarray(v, np.float32).reshape(32, 128).T)
    phct_h = cols(np.asarray(L0_attn_phase, np.float32).reshape(4096))
    bqc_h = cols(L0_attn_bq)
    bkc_h = cols(L0_attn_bk)
    bvc_h = cols(L0_attn_bv)
    bo_h = np.ascontiguousarray(np.asarray(L0_attn_bo, np.float32).reshape(1, 1024))
    sel_h = np.zeros((128, 256), dtype=ml_dtypes.bfloat16)
    for p in range(16):
        sel_h[:, p * 16 + p] = 1
    eyeb_h = np.eye(128, dtype=ml_dtypes.bfloat16)
    eyef_h = np.eye(128, dtype=np.float32)
    ones_h = np.ones((1, 128), dtype=ml_dtypes.bfloat16)

    if "nc" not in _prog_cache:
        _prog_cache["nc"] = build_program()
    nc = _prog_cache["nc"]

    in_maps = []
    for core in range(8):
        b, c2 = core // 4, core % 4
        xf = np.ascontiguousarray(x[b, 2048 * c2: 2048 * (c2 + 1), :])
        strips = [x[b, h * 2048 + 512 * c2: h * 2048 + 512 * (c2 + 1), :] for h in range(4)]
        xrr = np.ascontiguousarray(np.concatenate(strips, axis=0))
        in_maps.append({
            "xf": xf, "xr": xrr, "wq_p": wq_p, "wk_p": wk_p, "wv_p": wv_p,
            "wo_h": wo_h, "w1t": w1t, "w2t": w2t, "wrt_h": wrt_h, "phct_h": phct_h,
            "bqc_h": bqc_h, "bkc_h": bkc_h, "bvc_h": bvc_h, "bo_h": bo_h,
            "sel_h": sel_h, "eyeb_h": eyeb_h, "eyef_h": eyef_h, "ones_h": ones_h,
        })

    res = run_bass_kernel_spmd(nc, in_maps, core_ids=list(range(8)))
    out = np.empty((2, 8192, 1024), dtype=np.float32)
    for core in range(8):
        b, c2 = core // 4, core % 4
        r = np.asarray(res.results[core]["out"]).reshape(4, 512, 1024)
        for h in range(4):
            out[b, h * 2048 + 512 * c2: h * 2048 + 512 * (c2 + 1), :] = r[h]
    return out
